# version 09 (final)
# baseline (speedup 1.0000x reference)
import math

import ml_dtypes
import numpy as np

import concourse.bass as bass
import concourse.mybir as mybir
from concourse.bass_utils import run_bass_kernel_spmd

F32 = mybir.dt.float32
BF16 = mybir.dt.bfloat16
AF = mybir.ActivationFunctionType
ALU = mybir.AluOpType
AX = mybir.AxisListType

NCORES = 8
SEQS = 2
T = 2048
D = 1024
NT = T // 128
NC = D // 128
H = 16
DH = 4096
LN_EPS = 1e-5
NEG = -1.0e38


_HOST = {}


def _host_tables():
    if _HOST:
        return _HOST
    t = np.arange(T, dtype=np.float64)

    i = (np.arange(D) % 64) // 2
    f0 = 10000.0 ** (-(2.0 * i) / 64)
    ang0 = t[:, None] * f0[None, :]
    sgn0 = np.where(np.arange(D) % 2 == 0, -1.0, 1.0)
    _HOST["rope0_c"] = np.ascontiguousarray(np.cos(ang0), dtype=np.float32)
    _HOST["rope0_s"] = np.ascontiguousarray(np.sin(ang0) * sgn0[None, :],
                                            dtype=np.float32)

    j = np.arange(D) % (D // 2)
    f1 = 15000.0 ** (-(2.0 * j) / D)
    ang1 = f1[:, None] * t[None, :]
    sgn1 = np.where(np.arange(D) < D // 2, -1.0, 1.0)
    _HOST["rope1_c"] = np.ascontiguousarray(np.cos(ang1), dtype=np.float32)
    _HOST["rope1_s"] = np.ascontiguousarray(np.sin(ang1) * sgn1[:, None],
                                            dtype=np.float32)

    idx = np.arange(T)
    cnt = np.zeros((H, T, T), dtype=np.float32)
    for X in range(int(math.log2(T))):
        S_, r = 2 ** (X + 2), 2 ** X
        same = (idx[:, None] // S_) == (idx[None, :] // S_)
        for h in range(H):
            ok = (idx % r) == (h % r)
            cnt[h] += (same & ok[:, None] & ok[None, :]).astype(np.float32)
    causal = idx[None, :] <= idx[:, None]
    valid = (cnt > 0) & causal[None, :, :]
    v = valid.reshape(H, 16, 128, 16, 128).any(axis=(2, 4))
    _HOST["ne"] = v & np.tril(np.ones((16, 16), bool))[None]
    _HOST["cnt"] = np.ascontiguousarray(cnt.reshape(H * T, T)).astype(ml_dtypes.bfloat16)
    _HOST["cbias"] = np.where(valid, 0.0, NEG).reshape(H * T, T).astype(ml_dtypes.bfloat16)

    _HOST["identf"] = np.eye(128, dtype=np.float32)
    _HOST["trib"] = np.where(idx[None, :128] <= idx[:128, None], 0.0,
                             NEG).astype(ml_dtypes.bfloat16)
    return _HOST


def _prep_weights(inp):
    w = {}
    ct = lambda a: np.ascontiguousarray(a, dtype=np.float32)
    for nm in ("wq", "wk", "wv", "wo"):
        w["a0_" + nm] = ct(np.asarray(inp[f"L0_attn_{nm}"]).T)
    for src, nm in (("Wq", "wq"), ("Wk", "wk"), ("Wv", "wv"), ("Wo", "wo")):
        w["a1_" + nm] = ct(np.asarray(inp[f"L1_attn_{src}"]).T)
    for nm in ("bq", "bk", "bv", "bo"):
        b = np.asarray(inp[f"L1_attn_{nm}"])
        w["a1_" + nm + "c"] = ct(b.reshape(NC, 128).T)
        w["a1_" + nm + "b"] = ct(np.broadcast_to(b, (128, D)))
    w["f0_mu"] = ct(np.asarray(inp["L0_ffn_mu"]).reshape(D, 256))
    w["f0_gb"] = ct(np.broadcast_to(np.asarray(inp["L0_ffn_ln_gamma"]), (128, D)))
    w["f0_bb"] = ct(np.broadcast_to(np.asarray(inp["L0_ffn_ln_beta"]), (128, D)))
    w["f0_sc"] = ct(np.broadcast_to(np.asarray(inp["L0_ffn_scale"]).reshape(1, 1),
                                    (128, 1)))
    w["f0_w1"] = ct(np.asarray(inp["L0_ffn_w1"]).reshape(16 * D, 256))
    w["f0_w2"] = ct(np.asarray(inp["L0_ffn_w2"]).reshape(16 * 256, D))
    b1 = np.asarray(inp["L0_ffn_b1"])
    b2 = np.asarray(inp["L0_ffn_b2"])
    w["f0_b1b"] = ct(np.broadcast_to(b1[:, None, :], (16, 16, 256)).reshape(256, 256))
    w["f0_b2b"] = ct(np.broadcast_to(b2[:, None, :], (16, 16, D)).reshape(256, D))

    def chunk_pm(wt, n_m, n_c):
        a = wt.reshape(n_c, 128, n_m, 128)
        return ct(a.transpose(2, 1, 0, 3)).reshape(n_m * 128, n_c * 128)

    w["f1_w1"] = chunk_pm(np.asarray(inp["L1_ffn_w1_w"]).T, 32, 8)
    w["f1_w2"] = chunk_pm(np.asarray(inp["L1_ffn_w2_w"]).T, 32, 8)
    w["f1_h1"] = chunk_pm(np.asarray(inp["L1_ffn_h1_w"]).T, 32, 32)
    w["f1_h2"] = chunk_pm(np.asarray(inp["L1_ffn_h2_w"]).T, 32, 32)
    w["f1_w3"] = ct(np.asarray(inp["L1_ffn_w3_w"]).T)
    for nm in ("w1_s", "w2_s", "h1_s", "h2_s"):
        w["f1_" + nm] = ct(np.asarray(inp[f"L1_ffn_{nm}"]).reshape(32, 128).T)
    w["f1_w3_sb"] = ct(np.broadcast_to(np.asarray(inp["L1_ffn_w3_s"]), (128, D)))
    return w


class Prog:
    ENGN = ("pe", "act", "dve", "pool", "sp")

    NSEM = {"g": 24, "s": 24, "a": 4}

    def __init__(self, nc, ctx):
        self.nc = nc
        self.sem = {}
        self.count = {}
        self.waited = {n: {} for n in self.ENGN}
        self.ops = {n: [] for n in self.ENGN}
        self.res = {}
        self.sbuf_bytes = 0
        for n in self.ENGN:
            self.sem[n] = ctx.enter_context(nc.semaphore(n + "_sem"))
            self.count[n] = 0
        self.dring = {}
        self.dcount = {}
        for q, n in self.NSEM.items():
            for i in range(n):
                nm = f"dma_{q}{i}"
                self.sem[nm] = ctx.enter_context(nc.semaphore(nm))
                self.count[nm] = 0
            self.dring[q] = n
            self.dcount[q] = 0
        self._pending = {n: [] for n in self.ENGN}

    def sbuf(self, name, shape, dtype):
        per_part = int(np.prod(shape[1:])) * np.dtype(mybir.dt.np(dtype)).itemsize
        self.sbuf_bytes += per_part
        return self.nc.alloc_sbuf_tensor(name, shape, dtype)

    def _deps(self, reads, writes):
        deps = []
        for k in reads:
            e = self.res.get(k)
            if e and e["w"]:
                deps.append(e["w"])
        for k in writes:
            e = self.res.get(k)
            if e:
                if e["w"]:
                    deps.append(e["w"])
                deps.extend(e["r"])
        return deps

    def _wait(self, ename, deps):
        w = self.waited[ename]
        for (sname, val) in deps:
            if val <= 0 or w.get(sname, 0) >= val:
                continue
            w[sname] = val
            sem = self.sem[sname]
            self.ops[ename].append(lambda api, s=sem, v=val: api.wait_ge(s, v))

    def _update(self, reads, writes, tok):
        for k in writes:
            self.res[k] = {"w": tok, "r": []}
        for k in reads:
            self.res.setdefault(k, {"w": None, "r": []})["r"].append(tok)

    def op(self, ename, fn, reads=(), writes=(), inc=True):
        self._wait(ename, self._deps(reads, writes))
        if inc:
            self.count[ename] += 1
            val = self.count[ename]
            sem = self.sem[ename]
            self.ops[ename].append(lambda api, f=fn, s=sem: f(api).then_inc(s, 1))
            tok = (ename, val)
            for (kr, kw) in self._pending[ename]:
                self._update(kr, kw, tok)
            self._pending[ename] = []
            self._update(reads, writes, tok)
        else:
            self.ops[ename].append(lambda api, f=fn: f(api))
            self._pending[ename].append((list(reads), list(writes)))

    def dma(self, q, out, in_, reads=(), writes=(), **kw):
        ename = {"g": "pool", "s": "sp", "a": "act"}[q]
        self._wait(ename, self._deps(reads, writes))
        idx = self.dcount[q] % self.dring[q]
        self.dcount[q] += 1
        sname = f"dma_{q}{idx}"
        self.count[sname] += 16
        val = self.count[sname]
        sem = self.sem[sname]
        self.ops[ename].append(
            lambda api, o=out, i=in_, s=sem, k=dict(kw):
            api.dma_start(out=o, in_=i, **k).then_inc(s, 16))
        self._update(reads, writes, (sname, val))

    def barrier(self):
        toks = [(n, self.count[n]) for n in self.ENGN]
        for q, n_ in self.NSEM.items():
            toks += [(f"dma_{q}{i}", self.count[f"dma_{q}{i}"])
                     for i in range(n_)]
        for n in self.ENGN:
            self._wait(n, toks)

    def finalize(self, block):
        assert all(not p for p in self._pending.values()), "pending un-inc'd ops"
        api_of = {"pe": block.tensor, "act": block.scalar, "dve": block.vector,
                  "pool": block.gpsimd, "sp": block.sync}
        for name in self.ENGN:
            ops = self.ops[name]
            if not ops:
                continue
            def body(api, ops=ops):
                for f in ops:
                    f(api)
            api_of[name](body)


class Core:
    pass


def emit_ln_tile(P, B, src_key, src_ap, dst_ap, dst_key, affine=None, sq=0):
    st = B.stats
    c = 8 * sq
    k = [f"st{sq}"]
    P.op("dve", lambda v: v.scalar_tensor_tensor(
        dst_ap, src_ap, 1.0, src_ap, ALU.mult, ALU.mult,
        accum_out=st[:, c + 0:c + 1]), reads=[src_key], writes=k + [dst_key])
    P.op("dve", lambda v: v.reduce_sum(st[:, c + 1:c + 2], src_ap, axis=AX.X),
         reads=[src_key], writes=k)
    P.op("dve", lambda v: v.tensor_scalar(st[:, c + 2:c + 3], st[:, c + 1:c + 2],
                                          1.0 / D, None, ALU.mult),
         reads=k, writes=k)
    P.op("dve", lambda v: v.scalar_tensor_tensor(
        st[:, c + 3:c + 4], st[:, c + 2:c + 3], -1.0, st[:, c + 2:c + 3],
        ALU.mult, ALU.mult), reads=k, writes=k)
    P.op("dve", lambda v: v.scalar_tensor_tensor(
        st[:, c + 3:c + 4], st[:, c + 0:c + 1], 1.0 / D, st[:, c + 3:c + 4],
        ALU.mult, ALU.add), reads=k, writes=k)
    P.op("dve", lambda v: v.tensor_scalar(st[:, c + 3:c + 4], st[:, c + 3:c + 4],
                                          LN_EPS, None, ALU.add),
         reads=k, writes=k)
    P.op("act", lambda s: s.activation(st[:, c + 4:c + 5], st[:, c + 3:c + 4],
                                       AF.Ln, bias=0.0, scale=1.0),
         reads=k, writes=k)
    P.op("act", lambda s: s.activation(st[:, c + 4:c + 5], st[:, c + 4:c + 5],
                                       AF.Exp, bias=0.0, scale=-0.5),
         reads=k, writes=k)
    P.op("dve", lambda v: v.scalar_tensor_tensor(
        st[:, c + 5:c + 6], st[:, c + 2:c + 3], -1.0, st[:, c + 4:c + 5],
        ALU.mult, ALU.mult), reads=k, writes=k)
    P.op("act", lambda s: s.activation(dst_ap, src_ap, AF.Identity,
                                       bias=st[:, c + 5:c + 6],
                                       scale=st[:, c + 4:c + 5]),
         reads=[src_key] + k, writes=[dst_key])
    if affine is not None:
        g_ap, g_key, b_ap, b_key = affine
        P.op("dve", lambda v: v.tensor_tensor(dst_ap, dst_ap, g_ap, ALU.mult),
             reads=[dst_key, g_key], writes=[dst_key])
        P.op("dve", lambda v: v.tensor_tensor(dst_ap, dst_ap, b_ap, ALU.add),
             reads=[dst_key, b_key], writes=[dst_key])


def emit_transpose8(P, B, src_ap_fn, src_key, dst_ap_fn, dst_key, bank=6):
    ps = B.ps[bank]
    bk = f"ps{bank}"
    for g in range(2):
        for j in range(4):
            c = g * 4 + j
            P.op("pe", lambda t, c=c, j=j: t.transpose(
                ps[:, j * 128:(j + 1) * 128], src_ap_fn(c), B.identf[:]),
                reads=[src_key, "identf"], writes=[bk], inc=(j == 3))
        for j in range(4):
            c = g * 4 + j
            P.op("act", lambda s, c=c, j=j: s.copy(
                dst_ap_fn(c), ps[:, j * 128:(j + 1) * 128]),
                reads=[bk], writes=[dst_key])


def emit_load_x(P, B, seq):
    src = B.xin_ap[seq * T:(seq + 1) * T, :].rearrange("(t p) d -> p t d", p=128)
    dst = B.xsb[:].rearrange("p (t d) -> p t d", d=D)
    P.dma("s", dst, src, reads=[], writes=["xsb"])


def emit_store_x(P, B, seq):
    dst = B.xout_ap[seq * T:(seq + 1) * T, :].rearrange("(t p) d -> p t d", p=128)
    src = B.xsb[:].rearrange("p (t d) -> p t d", d=D)
    P.dma("s", dst, src, reads=["xsb"], writes=["xout"])


def emit_ln2_residual(P, B, tile):
    emit_ln_tile(P, B, "ytile", B.ytile[:], B.aux[:], "aux", sq=tile % 4)
    P.op("dve", lambda v, tile=tile: v.tensor_tensor(
        B.xsb[:, tile * D:(tile + 1) * D], B.xsb[:, tile * D:(tile + 1) * D],
        B.aux[:], ALU.add), reads=["aux", "xsb"], writes=["xsb"])


def wkeys(lo, hi):
    return [f"wslab{q}" for q in range(4) if lo < (q + 1) * 1024 and hi > q * 1024]


def skeys(lo, hi):
    return [f"ssb{q}" for q in range(4) if lo < (q + 1) * 512 and hi > q * 512]


def stkeys(lo, hi):
    return [f"st{q}" for q in range(4) if lo < (q + 1) * 8 and hi > q * 8]


def emit_xnw_window(P, B, w):
    for tt in range(4):
        tile = w * 4 + tt
        src = B.xsb[:, tile * D:(tile + 1) * D]
        dstb, dstk = (B.ytile, "ytile") if tt % 2 == 0 else (B.aux, "aux")
        emit_ln_tile(P, B, "xsb", src, dstb[:], dstk, sq=tt)
        emit_transpose8(
            P, B, lambda c, dstb=dstb: dstb[:, c * 128:(c + 1) * 128], dstk,
            lambda c, tt=tt: B.xnw[:, c * 512 + tt * 128:c * 512 + (tt + 1) * 128],
            "xnw", bank=6 + tt % 2)


def emit_attn(P, B, layer):
    emit_attn_qkv(P, B, layer)
    emit_attn_scores(P, B, layer)
    emit_attn_oproj(P, B, layer)


def emit_attn_qkv(P, B, layer, tensors=(0, 1, 2)):
    a = f"a{layer}"
    for ti, wn in enumerate(("wq", "wk", "wv")):
        if ti not in tensors:
            continue
        wt_ap = B.aps[f"{a}_{wn}"]
        dst, dst_key = [(B.qa, "qa"), (B.kv, "kv"), (B.vv, "vv")][ti]
        if ti == 2 and layer == 1:
            P.dma("s", B.bcast[:], B.aps["a1_bvb"], reads=[], writes=["bcast"])
        for w in range(4):
            emit_xnw_window(P, B, w)
            fm_out = (ti < 2 and layer == 1)
            if ti < 2 and layer == 0:
                for sub in range(2):
                    for c in range(NC):
                        q = c % 2
                        P.dma("g", B.wslab[:, q * 1024:(q + 1) * 1024],
                              wt_ap[c * 128:(c + 1) * 128, :],
                              reads=[], writes=[f"wslab{q}"])
                        for k in range(4):
                            tt, hf = k // 2, k % 2
                            tl = sub * 2 + tt
                            P.op("pe", lambda t, c=c, tl=tl, hf=hf, pi=k, q=q:
                                 t.matmul(
                                     B.ps[pi][:],
                                     B.xnw[:, c * 512 + tl * 128:
                                           c * 512 + (tl + 1) * 128],
                                     B.wslab[:, q * 1024 + hf * 512:
                                             q * 1024 + (hf + 1) * 512],
                                     start=(c == 0), stop=(c == NC - 1)),
                                 reads=[f"wslab{q}", "xnw"], writes=[f"ps{k}"],
                                 inc=(k == 3))
                    for tt in range(2):
                        tile = w * 4 + sub * 2 + tt
                        for hf in range(2):
                            P.op("act", lambda s, tt=tt, hf=hf: s.copy(
                                B.ytile[:, hf * 512:(hf + 1) * 512],
                                B.ps[tt * 2 + hf][:]),
                                reads=[f"ps{tt * 2 + hf}"], writes=["ytile"])
                        P.dma("s", B.bcast[:],
                              B.aps["rope0_c"][tile * 128:(tile + 1) * 128, :],
                              reads=[], writes=["bcast"])
                        P.dma("s", B.ropeb[:],
                              B.aps["rope0_s"][tile * 128:(tile + 1) * 128, :],
                              reads=[], writes=["ropeb", "ropebB"])
                        P.op("dve", lambda v: v.tensor_tensor(
                            B.ssb[:, 0:1024], B.ytile[:], B.bcast[:], ALU.mult),
                            reads=["ytile", "bcast"], writes=["ssb0", "ssb1"])
                        yv = B.ytile[:].rearrange("p (d two) -> p d two", two=2)
                        sv = B.ropeb[:].rearrange("p (d two) -> p d two", two=2)
                        t2 = B.ssb[:, 1024:2048].rearrange("p (d two) -> p d two",
                                                           two=2)
                        P.op("dve", lambda v, yv=yv, sv=sv, t2=t2: v.tensor_tensor(
                            t2[:, :, 0], yv[:, :, 1], sv[:, :, 0], ALU.mult),
                            reads=["ytile", "ropeb", "ropebB"],
                            writes=["ssb2", "ssb3"])
                        P.op("dve", lambda v, yv=yv, sv=sv, t2=t2: v.tensor_tensor(
                            t2[:, :, 1], yv[:, :, 0], sv[:, :, 1], ALU.mult),
                            reads=["ytile", "ropeb", "ropebB"],
                            writes=["ssb2", "ssb3"])
                        P.op("dve", lambda v: v.tensor_tensor(
                            B.aux[:], B.ssb[:, 0:1024], B.ssb[:, 1024:2048],
                            ALU.add), reads=["ssb0", "ssb1", "ssb2", "ssb3"],
                            writes=["aux"])
                        emit_transpose8(
                            P, B, lambda c: B.aux[:, c * 128:(c + 1) * 128], "aux",
                            lambda c, tile=tile, dst=dst: dst[:, c * T + tile * 128:
                                                              c * T + (tile + 1) * 128],
                            dst_key)
                continue
            for c in range(NC):
                q = c % 2
                P.dma("g", B.wslab[:, q * 1024:(q + 1) * 1024],
                      wt_ap[c * 128:(c + 1) * 128, :],
                      reads=[], writes=[f"wslab{q}"])
                for k in range(8):
                    if fm_out:
                        P.op("pe", lambda t, c=c, m=k, q=q: t.matmul(
                            B.ps[m][:],
                            B.wslab[:, q * 1024 + m * 128:q * 1024 + (m + 1) * 128],
                            B.xnw[:, c * 512:(c + 1) * 512],
                            start=(c == 0), stop=(c == NC - 1)),
                            reads=[f"wslab{q}", "xnw"], writes=[f"ps{k}"],
                            inc=(k == 7))
                    else:
                        tt, hf = k // 2, k % 2
                        P.op("pe", lambda t, c=c, tt=tt, hf=hf, pi=k, q=q: t.matmul(
                            B.ps[pi][:],
                            B.xnw[:, c * 512 + tt * 128:c * 512 + (tt + 1) * 128],
                            B.wslab[:, q * 1024 + hf * 512:q * 1024 + (hf + 1) * 512],
                            start=(c == 0), stop=(c == NC - 1)),
                            reads=[f"wslab{q}", "xnw"], writes=[f"ps{k}"],
                            inc=(k == 7))
            if ti < 2 and layer == 1:
                for m in range(8):
                    P.dma("s", B.ropeb[:, 0:512],
                          B.aps["rope1_c"][m * 128:(m + 1) * 128,
                                           w * 512:(w + 1) * 512],
                          reads=[], writes=["ropeb"])
                    P.dma("s", B.ropeb[:, 512:1024],
                          B.aps["rope1_s"][m * 128:(m + 1) * 128,
                                           w * 512:(w + 1) * 512],
                          reads=[], writes=["ropebB"])
                    P.op("dve", lambda v, m=m, ti=ti: v.scalar_tensor_tensor(
                        B.ssb[:, 0:512], B.ps[m][:],
                        B.bcol[:, ti * 8 + m:ti * 8 + m + 1],
                        B.ropeb[:, 0:512], ALU.add, ALU.mult),
                        reads=[f"ps{m}", "ropeb", "bcol"], writes=["ssb0"])
                    P.op("dve", lambda v, m=m, ti=ti: v.scalar_tensor_tensor(
                        B.ssb[:, 512:1024], B.ps[m ^ 4][:],
                        B.bcol[:, ti * 8 + (m ^ 4):ti * 8 + (m ^ 4) + 1],
                        B.ropeb[:, 512:1024], ALU.add, ALU.mult),
                        reads=[f"ps{m ^ 4}", "ropebB", "bcol"], writes=["ssb1"])
                    P.op("dve", lambda v, m=m, w=w, dst=dst: v.tensor_tensor(
                        dst[:, m * T + w * 512:m * T + (w + 1) * 512],
                        B.ssb[:, 0:512], B.ssb[:, 512:1024], ALU.add),
                        reads=["ssb0", "ssb1"], writes=[dst_key])
            elif ti == 2:
                for tt in range(4):
                    tile = w * 4 + tt
                    for hf in range(2):
                        pi = tt * 2 + hf
                        vdst = B.vv[:, tile * D + hf * 512:tile * D + (hf + 1) * 512]
                        if layer == 1:
                            P.op("dve", lambda v, pi=pi, vdst=vdst, hf=hf:
                                 v.tensor_tensor(vdst, B.ps[pi][:],
                                                 B.bcast[:, hf * 512:(hf + 1) * 512],
                                                 ALU.add),
                                 reads=[f"ps{pi}", "bcast"], writes=["vv"])
                        else:
                            P.op("act", lambda s, pi=pi, vdst=vdst: s.copy(
                                vdst, B.ps[pi][:]),
                                reads=[f"ps{pi}"], writes=["vv"])


def emit_attn_scores(P, B, layer):
    ne = _HOST.get("ne")
    st = B.stats
    scale = 1.0 if layer == 1 else 0.125
    pwbufs = [(B.pw[:, 0:512], "pw"), (B.ropeb[:, 0:512], "ropeb"),
              (B.ropeb[:, 512:1024], "ropebB"), (B.bcast[:, 0:512], "bcast")]

    def win_plan(qt, h):
        wins = []
        for j in range((qt + 1 + 3) // 4):
            wid = min(512, (qt + 1) * 128 - j * 512)
            tiles = list(range(wid // 128))
            if layer == 1:
                tiles = [i for i in tiles if ne[h, qt, j * 4 + i]]
                if not tiles:
                    continue
            wins.append((j, wid, tiles))
        return wins

    def cnt_slot(h):
        if h % 2 == 0:
            return (B.wslab[:, 0:2048], ["wslab0", "wslab1"],
                    B.wslab[:, 2048:4096], ["wslab2", "wslab3"])
        return (B.xnw[:, 0:2048], ["xnw"], B.xnw[:, 2048:4096], ["xnw"])

    def stage_a0(qt, h, wins):
        c = h // 2
        base = 64 * (h % 2)
        cntb, cntk, biasb, biask = cnt_slot(h)
        if layer == 1:
            for (j, wid, tiles) in wins:
                P.dma("s", cntb[:, j * 512:j * 512 + wid],
                      B.aps["cnt"][h * T + qt * 128:h * T + (qt + 1) * 128,
                                   j * 512:j * 512 + wid],
                      reads=[], writes=cntk)
                P.dma("s", biasb[:, j * 512:j * 512 + wid],
                      B.aps["cbias"][h * T + qt * 128:h * T + (qt + 1) * 128,
                                     j * 512:j * 512 + wid],
                      reads=[], writes=biask)
        for (j, wid, tiles) in wins:
            P.op("pe", lambda t, j=j, wid=wid, c=c, base=base, qt=qt: t.matmul(
                B.ps[j][:, 0:wid],
                B.qa[base:base + 64, c * T + qt * 128:c * T + (qt + 1) * 128],
                B.kv[base:base + 64, c * T + j * 512:c * T + j * 512 + wid],
                start=True, stop=True),
                reads=["qa", "kv"], writes=[f"ps{j}"])

    def stage_a1(qt, h, wins, sbase):
        kext = (qt + 1) * 128
        slot = h % 2
        stb = 16 * slot
        stq = ["st0", "st1"] if slot == 0 else ["st2", "st3"]
        cntb, cntk, biasb, biask = cnt_slot(h)
        mc = 0
        for wi, (j, wid, tiles) in enumerate(wins):
            sb = sbase + j * 512
            sk = skeys(sb, sb + wid)
            if layer == 1:
                P.op("dve", lambda v, j=j, wid=wid, sb=sb, cntb=cntb:
                     v.tensor_tensor(
                         B.ssb[:, sb:sb + wid], B.ps[j][:, 0:wid],
                         cntb[:, j * 512:j * 512 + wid], ALU.mult),
                     reads=[f"ps{j}"] + cntk, writes=sk)
                P.op("dve", lambda v, j=j, wid=wid, sb=sb, biasb=biasb:
                     v.tensor_tensor(
                         B.ssb[:, sb:sb + wid], B.ssb[:, sb:sb + wid],
                         biasb[:, j * 512:j * 512 + wid], ALU.add),
                     reads=sk + biask, writes=sk)
            else:
                P.op("act", lambda s, j=j, wid=wid, sb=sb: s.copy(
                    B.ssb[:, sb:sb + wid], B.ps[j][:, 0:wid]),
                    reads=[f"ps{j}"], writes=sk)
                if j == wins[-1][0]:
                    off = sbase + kext - 128
                    P.op("dve", lambda v, off=off: v.tensor_tensor(
                        B.ssb[:, off:off + 128], B.ssb[:, off:off + 128],
                        B.trib[:], ALU.add),
                        reads=sk + ["trib"], writes=sk)
            P.op("dve", lambda v, wid=wid, sb=sb, mc=mc, stb=stb: v.reduce_max(
                st[:, stb + mc:stb + mc + 1], B.ssb[:, sb:sb + wid],
                axis=AX.X), reads=sk, writes=stq)
            mc += 1
        P.op("dve", lambda v, mc=mc, stb=stb: v.reduce_max(
            st[:, stb + 8:stb + 9], st[:, stb:stb + mc], axis=AX.X),
            reads=stq, writes=stq)
        P.op("dve", lambda v, stb=stb: v.tensor_scalar(
            st[:, stb + 9:stb + 10], st[:, stb + 8:stb + 9], -scale, None,
            ALU.mult), reads=stq, writes=stq)

    def stage_bx(qt, h, wins, sbase):
        slot = h % 2
        stb = 16 * slot
        stq = ["st0", "st1"] if slot == 0 else ["st2", "st3"]
        pvb = 4 + slot
        ptp = 6 + slot
        ptbb = B.ptb[:, slot * 512:slot * 512 + 512]
        ptk = f"ptb{slot}"
        first = True
        last = (wins[-1][0], wins[-1][2][-1])
        for wi, (j, wid, tiles) in enumerate(wins):
            sb = sbase + j * 512
            sk = skeys(sb, sb + wid)
            pwb, pwk = pwbufs[slot * 2 + wi % 2]
            P.op("act", lambda s, wid=wid, sb=sb, wi=wi, stb=stb, pwb=pwb,
                 sk=sk: s.activation(
                     pwb[:, 0:wid], B.ssb[:, sb:sb + wid], AF.Exp,
                     bias=st[:, stb + 9:stb + 10], scale=scale),
                 reads=sk + stq, writes=[pwk])
            P.op("dve", lambda v, wid=wid, wi=wi, stb=stb, pwb=pwb:
                 v.reduce_sum(st[:, stb + 10 + wi:stb + 11 + wi],
                              pwb[:, 0:wid], axis=AX.X),
                 reads=[pwk], writes=stq)
            for ii, i in enumerate(tiles):
                P.op("pe", lambda t, i=i, ii=ii, pwb=pwb, pwk=pwk, ptp=ptp:
                     t.transpose(
                         B.ps[ptp][:, ii * 128:(ii + 1) * 128],
                         pwb[:, i * 128:(i + 1) * 128], B.identf[:]),
                     reads=[pwk, "identf"], writes=[f"ps{ptp}"],
                     inc=(ii == len(tiles) - 1))
            P.op("dve", lambda v, n=len(tiles), ptbb=ptbb, ptp=ptp:
                 v.tensor_copy(ptbb[:, 0:n * 128], B.ps[ptp][:, 0:n * 128]),
                 reads=[f"ps{ptp}"], writes=[ptk])
            for ii, i in enumerate(tiles):
                kt = j * 4 + i
                isl = (j, i) == last
                P.op("pe", lambda t, ii=ii, kt=kt, h=h, ptbb=ptbb, pvb=pvb,
                     first=first, isl=isl: t.matmul(
                         B.ps[pvb][:, 0:64],
                         ptbb[:, ii * 128:(ii + 1) * 128],
                         B.vv[:, kt * D + h * 64:kt * D + (h + 1) * 64],
                         start=first, stop=isl),
                     reads=[ptk, "vv"], writes=[f"ps{pvb}"], inc=isl)
                first = False
        nw = len(wins)
        P.op("dve", lambda v, nw=nw, stb=stb: v.reduce_sum(
            st[:, stb + 8:stb + 9], st[:, stb + 10:stb + 10 + nw], axis=AX.X),
            reads=stq, writes=stq)
        P.op("dve", lambda v, stb=stb: v.reciprocal(
            st[:, stb + 9:stb + 10], st[:, stb + 8:stb + 9]),
            reads=stq, writes=stq)
        P.op("act", lambda s, h=h, stb=stb, pvb=pvb: s.activation(
            B.aux[:, h * 64:(h + 1) * 64], B.ps[pvb][:, 0:64], AF.Identity,
            bias=0.0, scale=st[:, stb + 9:stb + 10]),
            reads=[f"ps{pvb}"] + stq, writes=["aux"])

    depth1 = False
    for qt in range(NT):
        kext = (qt + 1) * 128
        small = kext <= 1024
        prev = None
        for h in range(H):
            wins = win_plan(qt, h)
            sbase = (h % 2) * 1024 if small else 0
            stage_a0(qt, h, wins)
            if depth1:
                stage_a1(qt, h, wins, sbase)
                stage_bx(qt, h, wins, sbase)
                continue
            if small:
                stage_a1(qt, h, wins, sbase)
                if prev is not None:
                    stage_bx(*prev)
            else:
                if prev is not None:
                    stage_bx(*prev)
                stage_a1(qt, h, wins, sbase)
            prev = (qt, h, wins, sbase)
        if not depth1:
            stage_bx(*prev)
        emit_transpose8(
            P, B, lambda c: B.aux[:, c * 128:(c + 1) * 128], "aux",
            lambda c, qt=qt: B.qa[:, c * T + qt * 128:c * T + (qt + 1) * 128],
            "qa")


def emit_attn_oproj(P, B, layer):
    a = f"a{layer}"
    wo_ap = B.aps[f"{a}_wo"]
    if layer == 1:
        P.dma("s", B.bcast[:], B.aps["a1_bob"], reads=[], writes=["bcast"])
    for tb in range(4):
        for c in range(NC):
            q = c % 2
            P.dma("g", B.wslab[:, q * 1024:(q + 1) * 1024],
                  wo_ap[c * 128:(c + 1) * 128, :], reads=[], writes=[f"wslab{q}"])
            for k in range(8):
                tt, hf = k // 2, k % 2
                tile = tb * 4 + tt
                P.op("pe", lambda t, c=c, hf=hf, pi=k, tile=tile, q=q: t.matmul(
                    B.ps[pi][:],
                    B.qa[:, c * T + tile * 128:c * T + (tile + 1) * 128],
                    B.wslab[:, q * 1024 + hf * 512:q * 1024 + (hf + 1) * 512],
                    start=(c == 0), stop=(c == NC - 1)),
                    reads=[f"wslab{q}", "qa"], writes=[f"ps{k}"], inc=(k == 7))
        for tt in range(4):
            tile = tb * 4 + tt
            for hf in range(2):
                pi = tt * 2 + hf
                if layer == 1:
                    P.op("dve", lambda v, pi=pi, hf=hf: v.tensor_tensor(
                        B.ytile[:, hf * 512:(hf + 1) * 512], B.ps[pi][:],
                        B.bcast[:, hf * 512:(hf + 1) * 512], ALU.add),
                        reads=[f"ps{pi}", "bcast"], writes=["ytile"])
                else:
                    P.op("act", lambda s, pi=pi, hf=hf: s.copy(
                        B.ytile[:, hf * 512:(hf + 1) * 512], B.ps[pi][:]),
                        reads=[f"ps{pi}"], writes=["ytile"])
            emit_ln2_residual(P, B, tile)


def emit_f0(P, B):
    st = B.stats
    P.dma("s", B.ssb[:, 0:1024], B.aps["f0_gb"], reads=[], writes=["ssb0", "ssb1"])
    P.dma("s", B.ssb[:, 1024:2048], B.aps["f0_bb"], reads=[], writes=["ssb2", "ssb3"])
    P.dma("s", B.scol[:, 0:1], B.aps["f0_sc"], reads=[], writes=["scol"])
    for tile in range(NT):
        src = B.xsb[:, tile * D:(tile + 1) * D]
        emit_ln_tile(P, B, "xsb", src, B.ytile[:], "ytile")
        emit_ln_tile(P, B, "ytile", B.ytile[:], B.aux[:], "aux",
                     affine=(B.ssb[:, 0:1024], "ssb0", B.ssb[:, 1024:2048], "ssb2"))
        P.op("act", lambda s, tile=tile: s.activation(
            B.qa[:, tile * D:(tile + 1) * D], B.aux[:], AF.Identity, bias=0.0,
            scale=1.0), reads=["aux"], writes=["qa"])
        P.op("dve", lambda v: v.scalar_tensor_tensor(
            B.ytile[:], B.aux[:], 1.0, B.aux[:], ALU.mult, ALU.mult,
            accum_out=st[:, 0:1]), reads=["aux"], writes=["st0", "st1", "ytile"])
        P.op("dve", lambda v: v.tensor_scalar(st[:, 0:1], st[:, 0:1], 1e-30,
                                              None, ALU.add),
             reads=["st0", "st1"], writes=["st0", "st1"])
        P.op("act", lambda s: s.activation(st[:, 1:2], st[:, 0:1], AF.Ln,
                                           bias=0.0, scale=1.0),
             reads=["st0", "st1"], writes=["st0", "st1"])
        P.op("act", lambda s: s.activation(st[:, 1:2], st[:, 1:2], AF.Exp,
                                           bias=0.0, scale=0.5),
             reads=["st0", "st1"], writes=["st0", "st1"])
        P.op("dve", lambda v: v.tensor_scalar_max(st[:, 1:2], st[:, 1:2], 1e-12),
             reads=["st0", "st1"], writes=["st0", "st1"])
        P.op("dve", lambda v: v.reciprocal(st[:, 2:3], st[:, 1:2]),
             reads=["st0", "st1"], writes=["st0", "st1"])
        P.op("dve", lambda v: v.tensor_tensor(st[:, 3:4], st[:, 2:3],
                                              B.scol[:, 0:1], ALU.mult),
             reads=["st0", "st1", "scol"], writes=["st0", "st1"])
        P.op("act", lambda s: s.activation(B.ytile[:], B.aux[:], AF.Identity,
                                           bias=0.0, scale=st[:, 3:4]),
             reads=["aux", "st0", "st1"], writes=["ytile"])
        emit_transpose8(
            P, B, lambda c: B.ytile[:, c * 128:(c + 1) * 128], "ytile",
            lambda c, tile=tile: B.kv[:, c * T + tile * 128:c * T + (tile + 1) * 128],
            "kv")

    for c in range(NC):
        P.dma("s", B.aux[:, 0:256], B.aps["f0_mu"][c * 128:(c + 1) * 128, :],
              reads=["aux"], writes=["aux"])
        for half in range(2):
            P.op("pe", lambda t, half=half: t.transpose(
                B.ps[6][:, half * 128:(half + 1) * 128],
                B.aux[:, half * 128:(half + 1) * 128], B.identf[:]),
                reads=["aux", "identf"], writes=["ps6"], inc=(half == 1))
        for half in range(2):
            P.op("act", lambda s, c=c, half=half: s.copy(
                B.ssb[:, half * 1024 + c * 128:half * 1024 + (c + 1) * 128],
                B.ps[6][:, half * 128:(half + 1) * 128]),
                reads=["ps6"], writes=skeys(half * 1024, (half + 1) * 1024))
    for half in range(2):
        mt = B.ssb[:, half * 1024:(half + 1) * 1024]
        P.op("dve", lambda v, mt=mt: v.scalar_tensor_tensor(
            B.aux[:], mt, 1.0, mt, ALU.mult, ALU.mult, accum_out=st[:, 4:5]),
            reads=skeys(half * 1024, (half + 1) * 1024), writes=["st0", "st1", "aux"])
        P.op("dve", lambda v: v.tensor_scalar(st[:, 4:5], st[:, 4:5], 1e-30,
                                              None, ALU.add),
             reads=["st0", "st1"], writes=["st0", "st1"])
        P.op("act", lambda s: s.activation(st[:, 5:6], st[:, 4:5], AF.Ln,
                                           bias=0.0, scale=1.0),
             reads=["st0", "st1"], writes=["st0", "st1"])
        P.op("act", lambda s: s.activation(st[:, 5:6], st[:, 5:6], AF.Exp,
                                           bias=0.0, scale=0.5),
             reads=["st0", "st1"], writes=["st0", "st1"])
        P.op("dve", lambda v: v.tensor_scalar_max(st[:, 5:6], st[:, 5:6], 1e-12),
             reads=["st0", "st1"], writes=["st0", "st1"])
        P.op("dve", lambda v: v.reciprocal(st[:, 6:7], st[:, 5:6]),
             reads=["st0", "st1"], writes=["st0", "st1"])
        P.op("act", lambda s, mt=mt: s.activation(mt, mt, AF.Identity, bias=0.0,
                                                  scale=st[:, 6:7]),
             reads=skeys(half * 1024, (half + 1) * 1024) + ["st0", "st1"],
             writes=skeys(half * 1024, (half + 1) * 1024))
        for c in range(NC):
            P.op("pe", lambda t, half=half, c=c: t.transpose(
                B.ps[6][:, 0:128],
                B.ssb[:, half * 1024 + c * 128:half * 1024 + (c + 1) * 128],
                B.identf[:]),
                reads=skeys(half * 1024, (half + 1) * 1024) + ["identf"],
                writes=["ps6"])
            P.op("act", lambda s, half=half, c=c: s.copy(
                B.mub[:, c * 256 + half * 128:c * 256 + (half + 1) * 128],
                B.ps[6][:, 0:128]), reads=["ps6"], writes=["xnw"])

    for half in range(2):
        for w in range(4):
            for c in range(NC):
                P.op("pe", lambda t, half=half, w=w, c=c: t.matmul(
                    B.ps[w][:],
                    B.mub[:, c * 256 + half * 128:c * 256 + (half + 1) * 128],
                    B.kv[:, c * T + w * 512:c * T + (w + 1) * 512],
                    start=(c == 0), stop=(c == NC - 1)),
                    reads=["xnw", "kv"], writes=[f"ps{w}"], inc=(c == NC - 1))
        for w in range(4):
            P.op("dve", lambda v, w=w: v.reduce_max(
                st[:, 8 + w:9 + w], B.ps[w][:], axis=AX.X),
                reads=[f"ps{w}"], writes=["st1"])
        P.op("dve", lambda v: v.reduce_max(st[:, 12:13], st[:, 8:12], axis=AX.X),
             reads=["st0", "st1"], writes=["st0", "st1"])
        P.op("dve", lambda v: v.tensor_scalar(st[:, 13:14], st[:, 12:13], -1.0,
                                              None, ALU.mult),
             reads=["st0", "st1"], writes=["st0", "st1"])
        for w in range(4):
            P.op("act", lambda s, w=w: s.activation(
                B.ssb[:, w * 512:(w + 1) * 512], B.ps[w][:], AF.Exp,
                bias=st[:, 13:14], scale=1.0, accum_out=st[:, 8 + w:9 + w]),
                reads=[f"ps{w}", "st0", "st1"],
                writes=skeys(w * 512, (w + 1) * 512) + ["st0", "st1"])
        P.op("dve", lambda v: v.reduce_sum(st[:, 12:13], st[:, 8:12], axis=AX.X),
             reads=["st0", "st1"], writes=["st0", "st1"])
        P.op("dve", lambda v: v.reciprocal(st[:, 13:14], st[:, 12:13]),
             reads=["st0", "st1"], writes=["st0", "st1"])
        P.op("act", lambda s: s.activation(B.ssb[:], B.ssb[:], AF.Identity,
                                           bias=0.0, scale=st[:, 13:14]),
             reads=["ssb0", "ssb1", "ssb2", "ssb3", "st0", "st1"],
             writes=["ssb0", "ssb1", "ssb2", "ssb3"])
        for tile in range(NT):
            P.op("pe", lambda t, tile=tile: t.transpose(
                B.ps[6][:, 0:128], B.ssb[:, tile * 128:(tile + 1) * 128],
                B.identf[:]),
                reads=skeys(tile * 128, (tile + 1) * 128) + ["identf"],
                writes=["ps6"])
            P.op("act", lambda s, tile=tile, half=half: s.copy(
                B.vv[:, tile * 256 + half * 128:tile * 256 + (half + 1) * 128],
                B.ps[6][:, 0:128]), reads=["ps6"], writes=["vv"])

    for tile in range(NT):
        for c in range(NC):
            P.op("pe", lambda t, tile=tile, c=c: t.matmul(
                B.ps[0][:, 0:256],
                B.kv[:, c * T + tile * 128:c * T + (tile + 1) * 128],
                B.mub[:, c * 256:(c + 1) * 256],
                start=(c == 0), stop=(c == NC - 1)),
                reads=["kv", "xnw"], writes=["ps0"], inc=(c == NC - 1))
        P.op("dve", lambda v: v.reduce_max(st[:, 8:9], B.ps[0][:, 0:256],
                                           axis=AX.X),
             reads=["ps0"], writes=["st1"])
        P.op("dve", lambda v: v.tensor_scalar(st[:, 9:10], st[:, 8:9], -1.0,
                                              None, ALU.mult),
             reads=["st0", "st1"], writes=["st0", "st1"])
        P.op("act", lambda s: s.activation(B.aux[:, 0:256], B.ps[0][:, 0:256],
                                           AF.Exp, bias=st[:, 9:10], scale=1.0,
                                           accum_out=st[:, 8:9]),
             reads=["ps0", "st0", "st1"], writes=["aux", "st0", "st1"])
        P.op("dve", lambda v: v.reciprocal(st[:, 9:10], st[:, 8:9]),
             reads=["st0", "st1"], writes=["st0", "st1"])
        P.op("act", lambda s: s.activation(B.aux[:, 0:256], B.aux[:, 0:256],
                                           AF.Identity, bias=0.0, scale=st[:, 9:10]),
             reads=["aux", "st0", "st1"], writes=["aux"])
        for half in range(2):
            P.op("pe", lambda t, half=half: t.transpose(
                B.ps[6][:, 0:128], B.aux[:, half * 128:(half + 1) * 128],
                B.identf[:]), reads=["aux", "identf"], writes=["ps6"])
            P.op("act", lambda s, half=half, tile=tile: s.copy(
                B.vv[:, 4096 + half * 2048 + tile * 128:
                      4096 + half * 2048 + (tile + 1) * 128],
                B.ps[6][:, 0:128]), reads=["ps6"], writes=["vv"])

    for c in range(NC):
        pi = c % 2
        for tile in range(NT):
            P.op("pe", lambda t, c=c, tile=tile, pi=pi: t.matmul(
                B.ps[pi][:, 0:256],
                B.qa[:, tile * D + c * 128:tile * D + (c + 1) * 128],
                B.vv[:, tile * 256:(tile + 1) * 256],
                start=(tile == 0), stop=(tile == NT - 1)),
                reads=["qa", "vv"], writes=[f"ps{pi}"], inc=(tile == NT - 1))
        P.op("act", lambda s, c=c, pi=pi: s.copy(
            B.slotin[:, c * 256:(c + 1) * 256], B.ps[pi][:, 0:256]),
            reads=[f"ps{pi}"], writes=["xnw"])

    for e in range(16):
        for c in range(NC):
            lo = (c % 4) * 256 + (2048 if c >= 4 else 0)
            P.dma("g", B.wslab[:, lo:lo + 256],
                  B.aps["f0_w1"][e * D + c * 128:e * D + (c + 1) * 128, :],
                  reads=[], writes=wkeys(lo, lo + 256))
        for c in range(NC):
            lo = (c % 4) * 256 + (2048 if c >= 4 else 0)
            P.op("pe", lambda t, e=e, c=c, lo=lo: t.matmul(
                B.ps[2][0:16, 0:256],
                B.slotin[:, c * 256 + e * 16:c * 256 + (e + 1) * 16],
                B.wslab[:, lo:lo + 256],
                start=(c == 0), stop=(c == NC - 1)),
                reads=["xnw"] + wkeys(lo, lo + 256), writes=["ps2"],
                inc=(c == NC - 1))
        P.dma("s", B.hsb[0:16, 0:256], B.aps["f0_b1b"][e * 16:(e + 1) * 16, :],
              reads=["pw"], writes=["pw"])
        P.op("dve", lambda v: v.tensor_tensor(
            B.hsb[0:16, 0:256], B.ps[2][0:16, 0:256], B.hsb[0:16, 0:256],
            ALU.add), reads=["ps2", "pw"], writes=["pw"])
        P.op("act", lambda s: s.activation(B.hsb[0:16, 256:512],
                                           B.hsb[0:16, 0:256], AF.Gelu,
                                           bias=0.0, scale=1.0),
             reads=["pw"], writes=["pw"])
        for half in range(2):
            P.op("pe", lambda t, half=half: t.transpose(
                B.ps[3][:, 0:16],
                B.hsb[0:16, 256 + half * 128:256 + (half + 1) * 128],
                B.identf[0:16, 0:16]),
                reads=["pw", "identf"], writes=["ps3"])
            P.op("act", lambda s, half=half: s.copy(
                B.htb[:, half * 16:(half + 1) * 16], B.ps[3][:, 0:16]),
                reads=["ps3"], writes=["ptb0"])
        for half in range(2):
            lo = half * 1024
            P.dma("g", B.wslab[:, lo:lo + 1024],
                  B.aps["f0_w2"][e * 256 + half * 128:e * 256 + (half + 1) * 128, :],
                  reads=[], writes=wkeys(lo, lo + 1024))
        P.dma("s", B.aux[0:16, :], B.aps["f0_b2b"][e * 16:(e + 1) * 16, :],
              reads=["aux"], writes=["aux"])
        for dh in range(2):
            for half in range(2):
                lo = half * 1024
                P.op("pe", lambda t, dh=dh, half=half, lo=lo: t.matmul(
                    B.ps[2][0:16, 0:512],
                    B.htb[:, half * 16:(half + 1) * 16],
                    B.wslab[:, lo + dh * 512:lo + (dh + 1) * 512],
                    start=(half == 0), stop=(half == 1)),
                    reads=["ptb0"] + wkeys(lo, lo + 1024),
                    writes=["ps2"], inc=(half == 1))
            P.op("dve", lambda v, dh=dh: v.tensor_tensor(
                B.aux[0:16, dh * 512:(dh + 1) * 512], B.ps[2][0:16, 0:512],
                B.aux[0:16, dh * 512:(dh + 1) * 512], ALU.add),
                reads=["ps2", "aux"], writes=["aux"])
        P.dma("g", B.vv[16 * (e % 8):16 * (e % 8) + 16,
                        8192 + (e // 8) * 1024:8192 + (e // 8 + 1) * 1024],
              B.aux[0:16, 0:1024], reads=["aux"], writes=["vv"])

    for tile in range(NT):
        for dh in range(2):
            pi = dh
            for ch in range(2):
                P.op("pe", lambda t, tile=tile, dh=dh, ch=ch, pi=pi: t.matmul(
                    B.ps[pi][:],
                    B.vv[:, 4096 + ch * 2048 + tile * 128:
                          4096 + ch * 2048 + (tile + 1) * 128],
                    B.vv[:, 8192 + ch * 1024 + dh * 512:
                          8192 + ch * 1024 + (dh + 1) * 512],
                    start=(ch == 0), stop=(ch == 1)),
                    reads=["vv"], writes=[f"ps{pi}"], inc=(ch == 1))
            P.op("act", lambda s, dh=dh, pi=pi: s.copy(
                B.ytile[:, dh * 512:(dh + 1) * 512], B.ps[pi][:]),
                reads=[f"ps{pi}"], writes=["ytile"])
        emit_ln2_residual(P, B, tile)


def emit_f1_cast(P, B):
    for nm in ("w1", "w2", "h1", "h2", "w3"):
        src = B.aps[f"f1_{nm}"]
        dst = B.scr[nm]
        rows = src.shape[0]
        step = rows // 8
        for i in range(8):
            P.dma("g", dst[i * step:(i + 1) * step, :],
                  src[i * step:(i + 1) * step, :],
                  reads=[], writes=[f"scr_{nm}"])


def emit_f1(P, B):
    st = B.stats
    P.dma("s", B.scol[:, 0:32], B.aps["f1_w1_s"], reads=["scol"], writes=["scol"])
    P.dma("s", B.scol[:, 32:64], B.aps["f1_w2_s"], reads=["scol"], writes=["scol"])
    P.dma("s", B.scol2[:, 0:32], B.aps["f1_h1_s"], reads=["scol2"], writes=["scol2"])
    P.dma("s", B.scol2[:, 32:64], B.aps["f1_h2_s"], reads=["scol2"], writes=["scol2"])
    P.dma("s", B.bcast[:], B.aps["f1_w3_sb"], reads=["bcast"], writes=["bcast"])
    for w in range(4):
        emit_xnw_window(P, B, w)
        for m in range(32):
            par = m % 2
            off = 2048 * par
            k1, k2 = f"wslab{2 * par}", f"wslab{2 * par + 1}"
            sso = 1024 * par
            psa, psb = 2 * par, 2 * par + 1
            P.dma("g", B.wslab[:, off:off + 1024],
                  B.scr["w1"][m * 128:(m + 1) * 128, :],
                  reads=["scr_w1"], writes=[k1])
            P.dma("g", B.wslab[:, off + 1024:off + 2048],
                  B.scr["w2"][m * 128:(m + 1) * 128, :],
                  reads=["scr_w2"], writes=[k2])
            for c in range(NC):
                P.op("pe", lambda t, c=c, off=off, psa=psa, k1=k1: t.matmul(
                    B.ps[psa][:], B.wslab[:, off + c * 128:off + (c + 1) * 128],
                    B.xnw[:, c * 512:(c + 1) * 512],
                    start=(c == 0), stop=(c == NC - 1)),
                    reads=[k1, "xnw"], writes=[f"ps{psa}"], inc=(c == NC - 1))
                P.op("pe", lambda t, c=c, off=off, psb=psb, k2=k2: t.matmul(
                    B.ps[psb][:],
                    B.wslab[:, off + 1024 + c * 128:off + 1024 + (c + 1) * 128],
                    B.xnw[:, c * 512:(c + 1) * 512],
                    start=(c == 0), stop=(c == NC - 1)),
                    reads=[k2, "xnw"], writes=[f"ps{psb}"], inc=(c == NC - 1))
            P.op("act", lambda s, m=m, psa=psa, sso=sso: s.activation(
                B.ssb[:, sso:sso + 512], B.ps[psa][:], AF.Silu, bias=0.0,
                scale=B.scol[:, m:m + 1]),
                reads=[f"ps{psa}", "scol"], writes=skeys(sso, sso + 512))
            P.op("act", lambda s, m=m, psb=psb, sso=sso: s.activation(
                B.ssb[:, sso + 512:sso + 1024], B.ps[psb][:], AF.Identity,
                bias=0.0, scale=B.scol[:, 32 + m:33 + m]),
                reads=[f"ps{psb}", "scol"], writes=skeys(sso + 512, sso + 1024))
            P.op("dve", lambda v, m=m, sso=sso: v.tensor_tensor(
                B.qa[:, m * 512:(m + 1) * 512], B.ssb[:, sso:sso + 512],
                B.ssb[:, sso + 512:sso + 1024], ALU.mult),
                reads=skeys(sso, sso + 1024), writes=["qa"])
        for (wname, src_key, srcbuf, dst_key, dstbuf, scidx) in (
                ("h1", "qa", B.qa, "kv", B.kv, 0),
                ("h2", "kv", B.kv, "qa", B.qa, 32)):
            for m in range(32):
                par = m % 2
                slab = B.wslab if par == 0 else B.xnw
                slabk = (["wslab0", "wslab1", "wslab2", "wslab3"]
                         if par == 0 else ["xnw"])
                P.dma("g", slab[:, 0:4096],
                      B.scr[wname][m * 128:(m + 1) * 128, :],
                      reads=[f"scr_{wname}"], writes=slabk)
                pi = m % 2
                for i in range(32):
                    P.op("pe", lambda t, i=i, pi=pi, srcbuf=srcbuf, slab=slab,
                         slabk=slabk: t.matmul(
                             B.ps[pi][:], slab[:, i * 128:(i + 1) * 128],
                             srcbuf[:, i * 512:(i + 1) * 512],
                             start=(i == 0), stop=(i == 31)),
                        reads=slabk + [src_key],
                        writes=[f"ps{pi}"], inc=(i == 31))
                P.op("act", lambda s, m=m, pi=pi, dstbuf=dstbuf, scidx=scidx:
                     s.activation(dstbuf[:, m * 512:(m + 1) * 512], B.ps[pi][:],
                                  AF.Gelu, bias=0.0,
                                  scale=B.scol2[:, scidx + m:scidx + m + 1]),
                     reads=[f"ps{pi}", "scol2"], writes=[dst_key])
        for i in range(32):
            off = 1024 * (i % 2)
            wk = f"wslab{i % 2}"
            P.dma("g", B.wslab[:, off:off + 1024],
                  B.scr["w3"][i * 128:(i + 1) * 128, :],
                  reads=["scr_w3"], writes=[wk])
            for k in range(8):
                tt, hf = k // 2, k % 2
                P.op("pe", lambda t, i=i, tt=tt, hf=hf, pi=k, off=off, wk=wk:
                     t.matmul(
                         B.ps[pi][:],
                         B.qa[:, i * 512 + tt * 128:i * 512 + (tt + 1) * 128],
                         B.wslab[:, off + hf * 512:off + (hf + 1) * 512],
                         start=(i == 0), stop=(i == 31)),
                    reads=[wk, "qa"], writes=[f"ps{k}"], inc=(k == 7))
        for tt in range(4):
            tile = w * 4 + tt
            for hf in range(2):
                pi = tt * 2 + hf
                P.op("dve", lambda v, pi=pi, hf=hf: v.tensor_tensor(
                    B.ytile[:, hf * 512:(hf + 1) * 512], B.ps[pi][:],
                    B.bcast[:, hf * 512:(hf + 1) * 512], ALU.mult),
                    reads=[f"ps{pi}", "bcast"], writes=["ytile"])
            emit_ln2_residual(P, B, tile)


def build_nc(parts):
    import contextlib
    nc = bass.Bass("TRN2", target_bir_lowering=False)
    ctx = contextlib.ExitStack()
    P = Prog(nc, ctx)
    B = Core()
    B.aps = {}
    B.scr = {}

    def din(name, shape, dtype=F32):
        B.aps[name] = nc.dram_tensor(name, list(shape), dtype,
                                     kind="ExternalInput").ap()
        return B.aps[name]

    din("xin", (SEQS * T, D))
    B.xin_ap = B.aps["xin"]
    B.xout_ap = nc.dram_tensor("xout", [SEQS * T, D], F32,
                               kind="ExternalOutput").ap()
    din("identf", (128, 128))
    din("trib", (128, 128), BF16)
    if "a0" in parts:
        for nm in ("wq", "wk", "wv", "wo"):
            din(f"a0_{nm}", (D, D))
        din("rope0_c", (T, D))
        din("rope0_s", (T, D))
    if "a1" in parts:
        for nm in ("wq", "wk", "wv", "wo"):
            din(f"a1_{nm}", (D, D))
        for nm in ("bq", "bk", "bv", "bo"):
            din(f"a1_{nm}c", (128, NC))
        din("a1_bvb", (128, D))
        din("a1_bob", (128, D))
        din("rope1_c", (D, T))
        din("rope1_s", (D, T))
        din("cnt", (H * T, T), BF16)
        din("cbias", (H * T, T), BF16)
    if "f0" in parts:
        din("f0_mu", (D, 256))
        din("f0_gb", (128, D))
        din("f0_bb", (128, D))
        din("f0_sc", (128, 1))
        din("f0_w1", (16 * D, 256))
        din("f0_w2", (16 * 256, D))
        din("f0_b1b", (256, 256))
        din("f0_b2b", (256, D))
    if "f1" in parts:
        din("f1_w1", (32 * 128, 8 * 128))
        din("f1_w2", (32 * 128, 8 * 128))
        din("f1_h1", (32 * 128, 32 * 128))
        din("f1_h2", (32 * 128, 32 * 128))
        din("f1_w3", (DH, D))
        for nm in ("w1_s", "w2_s", "h1_s", "h2_s"):
            din(f"f1_{nm}", (128, 32))
        din("f1_w3_sb", (128, D))
        for nm, shape in (("w1", (32 * 128, 8 * 128)), ("w2", (32 * 128, 8 * 128)),
                          ("h1", (32 * 128, 32 * 128)),
                          ("h2", (32 * 128, 32 * 128)), ("w3", (DH, D))):
            B.scr[nm] = nc.dram_tensor("scr_" + nm, list(shape), BF16,
                                       kind="Internal").ap()

    B.xsb = P.sbuf("xsb", [128, NT * D], F32)
    B.qa = P.sbuf("qa", [128, 8 * T], BF16)
    B.kv = P.sbuf("kv", [128, 8 * T], BF16)
    B.vv = P.sbuf("vv", [128, NT * D], BF16)
    B.wslab = P.sbuf("wslab", [128, 4096], BF16)
    B.xnw = P.sbuf("xnw", [128, 4096], BF16)
    B.ropeb = P.sbuf("ropeb", [128, 1024], F32)
    B.ssb = P.sbuf("ssb", [128, 2048], F32)
    B.pw = P.sbuf("pw", [128, 512], F32)
    B.ptb = P.sbuf("ptb0", [128, 1024], BF16)
    B.ytile = P.sbuf("ytile", [128, 1024], F32)
    B.aux = P.sbuf("aux", [128, 1024], F32)
    B.stats = P.sbuf("stats", [128, 32], F32)
    B.bcast = P.sbuf("bcast", [128, 1024], F32)
    B.scol = P.sbuf("scol", [128, 64], F32)
    B.scol2 = P.sbuf("scol2", [128, 64], F32)
    B.hsb = B.pw
    B.htb = B.ptb
    B.mub = B.xnw
    B.slotin = B.mub
    B.identf = P.sbuf("identf_sb", [128, 128], F32)
    B.trib = P.sbuf("trib_sb", [128, 128], BF16)
    B.bcol = P.sbuf("bcol", [128, 16], F32)
    B.ps = [nc.alloc_psum_tensor(f"ps{i}", [128, 512], F32) for i in range(8)]
    assert P.sbuf_bytes <= 211000, f"SBUF over budget: {P.sbuf_bytes}"

    with nc.Block() as block:
        P.dma("s", B.identf[:], B.aps["identf"], writes=["identf"])
        P.dma("s", B.trib[:], B.aps["trib"], writes=["trib"])
        if "f1" in parts:
            emit_f1_cast(P, B)
        for seq in range(SEQS):
            emit_load_x(P, B, seq)
            for part in parts:
                if part == "a0":
                    emit_attn(P, B, 0)
                elif part == "a1":
                    P.dma("s", B.bcol[:, 0:8], B.aps["a1_bqc"], writes=["bcol"])
                    P.dma("s", B.bcol[:, 8:16], B.aps["a1_bkc"], writes=["bcol"])
                    emit_attn(P, B, 1)
                elif part == "f0":
                    emit_f0(P, B)
                elif part == "f1":
                    emit_f1(P, B)
            emit_store_x(P, B, seq)
        P.barrier()
        P.finalize(block)
    return nc, ctx


def _in_map_for(parts, w, tabs, xin):
    m = {"xin": xin, "identf": tabs["identf"], "trib": tabs["trib"]}
    if "a0" in parts:
        for nm in ("wq", "wk", "wv", "wo"):
            m[f"a0_{nm}"] = w[f"a0_{nm}"]
        m["rope0_c"] = tabs["rope0_c"]
        m["rope0_s"] = tabs["rope0_s"]
    if "a1" in parts:
        for nm in ("wq", "wk", "wv", "wo"):
            m[f"a1_{nm}"] = w[f"a1_{nm}"]
        for nm in ("bq", "bk", "bv", "bo"):
            m[f"a1_{nm}c"] = w[f"a1_{nm}c"]
        m["a1_bvb"] = w["a1_bvb"]
        m["a1_bob"] = w["a1_bob"]
        m["rope1_c"] = tabs["rope1_c"]
        m["rope1_s"] = tabs["rope1_s"]
        m["cnt"] = tabs["cnt"]
        m["cbias"] = tabs["cbias"]
    if "f0" in parts:
        for nm in ("mu", "gb", "bb", "sc", "w1", "w2", "b1b", "b2b"):
            m[f"f0_{nm}"] = w[f"f0_{nm}"]
    if "f1" in parts:
        for nm in ("w1", "w2", "h1", "h2", "w3", "w1_s", "w2_s", "h1_s",
                   "h2_s", "w3_sb"):
            m[f"f1_{nm}"] = w[f"f1_{nm}"]
    return m


def _run_parts(parts_list, w, tabs, x):
    cur = np.ascontiguousarray(x.reshape(16 * T, D), dtype=np.float32)
    for parts in parts_list:
        nc, ctx = build_nc(parts)
        in_maps = []
        for c in range(NCORES):
            xin = np.ascontiguousarray(cur[c * SEQS * T:(c + 1) * SEQS * T, :])
            in_maps.append(_in_map_for(parts, w, tabs, xin))
        res = run_bass_kernel_spmd(nc, in_maps, core_ids=list(range(NCORES)))
        ctx.close()
        cur = np.concatenate([res.results[c]["xout"] for c in range(NCORES)],
                             axis=0)
    return cur.reshape(16, T, D)


def kernel(**inputs):
    tabs = _host_tables()
    w = _prep_weights(inputs)
    x = np.asarray(inputs["L0_attn_x"], dtype=np.float32)
    out = _run_parts([("a0", "f0", "a1", "f1")], w, tabs, x)
    return out.astype(np.float32)


# version 11
# speedup vs baseline: 1.0141x; 1.0141x over previous
import math

import ml_dtypes
import numpy as np

import concourse.bass as bass
import concourse.mybir as mybir
from concourse.bass_utils import run_bass_kernel_spmd

F32 = mybir.dt.float32
BF16 = mybir.dt.bfloat16
AF = mybir.ActivationFunctionType
ALU = mybir.AluOpType
AX = mybir.AxisListType

NCORES = 8
SEQS = 2
T = 2048
D = 1024
NT = T // 128
NC = D // 128
H = 16
DH = 4096
LN_EPS = 1e-5
NEG = -1.0e38


_HOST = {}


def _host_tables():
    if _HOST:
        return _HOST
    t = np.arange(T, dtype=np.float64)

    i = (np.arange(D) % 64) // 2
    f0 = 10000.0 ** (-(2.0 * i) / 64)
    ang0 = t[:, None] * f0[None, :]
    sgn0 = np.where(np.arange(D) % 2 == 0, -1.0, 1.0)
    _HOST["rope0_c"] = np.ascontiguousarray(np.cos(ang0), dtype=np.float32)
    _HOST["rope0_s"] = np.ascontiguousarray(np.sin(ang0) * sgn0[None, :],
                                            dtype=np.float32)

    j = np.arange(D) % (D // 2)
    f1 = 15000.0 ** (-(2.0 * j) / D)
    ang1 = f1[:, None] * t[None, :]
    sgn1 = np.where(np.arange(D) < D // 2, -1.0, 1.0)
    _HOST["rope1_c"] = np.ascontiguousarray(np.cos(ang1), dtype=np.float32)
    _HOST["rope1_s"] = np.ascontiguousarray(np.sin(ang1) * sgn1[:, None],
                                            dtype=np.float32)

    idx = np.arange(T)
    cnt = np.zeros((H, T, T), dtype=np.float32)
    for X in range(int(math.log2(T))):
        S_, r = 2 ** (X + 2), 2 ** X
        same = (idx[:, None] // S_) == (idx[None, :] // S_)
        for h in range(H):
            ok = (idx % r) == (h % r)
            cnt[h] += (same & ok[:, None] & ok[None, :]).astype(np.float32)
    causal = idx[None, :] <= idx[:, None]
    valid = (cnt > 0) & causal[None, :, :]
    v = valid.reshape(H, 16, 128, 16, 128).any(axis=(2, 4))
    _HOST["ne"] = v & np.tril(np.ones((16, 16), bool))[None]
    _HOST["cnt"] = np.ascontiguousarray(cnt.reshape(H * T, T)).astype(ml_dtypes.bfloat16)
    _HOST["cbias"] = np.where(valid, 0.0, NEG).reshape(H * T, T).astype(ml_dtypes.bfloat16)

    _HOST["identf"] = np.eye(128, dtype=np.float32)
    _HOST["trib"] = np.where(idx[None, :128] <= idx[:128, None], 0.0,
                             NEG).astype(ml_dtypes.bfloat16)
    return _HOST


def _prep_weights(inp):
    w = {}
    ct = lambda a: np.ascontiguousarray(a, dtype=np.float32)
    for nm in ("wq", "wk", "wv", "wo"):
        w["a0_" + nm] = ct(np.asarray(inp[f"L0_attn_{nm}"]).T)
    for src, nm in (("Wq", "wq"), ("Wk", "wk"), ("Wv", "wv"), ("Wo", "wo")):
        w["a1_" + nm] = ct(np.asarray(inp[f"L1_attn_{src}"]).T)
    for nm in ("bq", "bk", "bv", "bo"):
        b = np.asarray(inp[f"L1_attn_{nm}"])
        w["a1_" + nm + "c"] = ct(b.reshape(NC, 128).T)
        w["a1_" + nm + "b"] = ct(np.broadcast_to(b, (128, D)))
    w["f0_mu"] = ct(np.asarray(inp["L0_ffn_mu"]).reshape(D, 256))
    w["f0_gb"] = ct(np.broadcast_to(np.asarray(inp["L0_ffn_ln_gamma"]), (128, D)))
    w["f0_bb"] = ct(np.broadcast_to(np.asarray(inp["L0_ffn_ln_beta"]), (128, D)))
    w["f0_sc"] = ct(np.broadcast_to(np.asarray(inp["L0_ffn_scale"]).reshape(1, 1),
                                    (128, 1)))
    w["f0_w1"] = ct(np.asarray(inp["L0_ffn_w1"]).reshape(16 * D, 256))
    w["f0_w2"] = ct(np.asarray(inp["L0_ffn_w2"]).reshape(16 * 256, D))
    b1 = np.asarray(inp["L0_ffn_b1"])
    b2 = np.asarray(inp["L0_ffn_b2"])
    w["f0_b1b"] = ct(np.broadcast_to(b1[:, None, :], (16, 16, 256)).reshape(256, 256))
    w["f0_b2b"] = ct(np.broadcast_to(b2[:, None, :], (16, 16, D)).reshape(256, D))

    def chunk_pm(wt, n_m, n_c):
        a = wt.reshape(n_c, 128, n_m, 128)
        return ct(a.transpose(2, 1, 0, 3)).reshape(n_m * 128, n_c * 128)

    w["f1_w1"] = chunk_pm(np.asarray(inp["L1_ffn_w1_w"]).T, 32, 8)
    w["f1_w2"] = chunk_pm(np.asarray(inp["L1_ffn_w2_w"]).T, 32, 8)
    w["f1_h1"] = chunk_pm(np.asarray(inp["L1_ffn_h1_w"]).T, 32, 32)
    w["f1_h2"] = chunk_pm(np.asarray(inp["L1_ffn_h2_w"]).T, 32, 32)
    w["f1_w3"] = ct(np.asarray(inp["L1_ffn_w3_w"]).T)
    for nm in ("w1_s", "w2_s", "h1_s", "h2_s"):
        w["f1_" + nm] = ct(np.asarray(inp[f"L1_ffn_{nm}"]).reshape(32, 128).T)
    w["f1_w3_sb"] = ct(np.broadcast_to(np.asarray(inp["L1_ffn_w3_s"]), (128, D)))
    return w


class Prog:
    ENGN = ("pe", "act", "dve", "pool", "sp")

    NSEM = {"g": 24, "s": 24, "a": 4}

    def __init__(self, nc, ctx):
        self.nc = nc
        self.sem = {}
        self.count = {}
        self.waited = {n: {} for n in self.ENGN}
        self.ops = {n: [] for n in self.ENGN}
        self.res = {}
        self.sbuf_bytes = 0
        for n in self.ENGN:
            self.sem[n] = ctx.enter_context(nc.semaphore(n + "_sem"))
            self.count[n] = 0
        self.dring = {}
        self.dcount = {}
        for q, n in self.NSEM.items():
            for i in range(n):
                nm = f"dma_{q}{i}"
                self.sem[nm] = ctx.enter_context(nc.semaphore(nm))
                self.count[nm] = 0
            self.dring[q] = n
            self.dcount[q] = 0
        self._pending = {n: [] for n in self.ENGN}

    def sbuf(self, name, shape, dtype):
        per_part = int(np.prod(shape[1:])) * np.dtype(mybir.dt.np(dtype)).itemsize
        self.sbuf_bytes += per_part
        return self.nc.alloc_sbuf_tensor(name, shape, dtype)

    def _deps(self, reads, writes):
        deps = []
        for k in reads:
            e = self.res.get(k)
            if e and e["w"]:
                deps.append(e["w"])
        for k in writes:
            e = self.res.get(k)
            if e:
                if e["w"]:
                    deps.append(e["w"])
                deps.extend(e["r"])
        return deps

    def _wait(self, ename, deps):
        w = self.waited[ename]
        for (sname, val) in deps:
            if val <= 0 or w.get(sname, 0) >= val:
                continue
            w[sname] = val
            sem = self.sem[sname]
            self.ops[ename].append(lambda api, s=sem, v=val: api.wait_ge(s, v))

    def _update(self, reads, writes, tok):
        for k in writes:
            self.res[k] = {"w": tok, "r": []}
        for k in reads:
            self.res.setdefault(k, {"w": None, "r": []})["r"].append(tok)

    def op(self, ename, fn, reads=(), writes=(), inc=True):
        self._wait(ename, self._deps(reads, writes))
        if inc:
            self.count[ename] += 1
            val = self.count[ename]
            sem = self.sem[ename]
            self.ops[ename].append(lambda api, f=fn, s=sem: f(api).then_inc(s, 1))
            tok = (ename, val)
            for (kr, kw) in self._pending[ename]:
                self._update(kr, kw, tok)
            self._pending[ename] = []
            self._update(reads, writes, tok)
        else:
            self.ops[ename].append(lambda api, f=fn: f(api))
            self._pending[ename].append((list(reads), list(writes)))

    def dma(self, q, out, in_, reads=(), writes=(), **kw):
        ename = {"g": "pool", "s": "sp", "a": "act"}[q]
        self._wait(ename, self._deps(reads, writes))
        idx = self.dcount[q] % self.dring[q]
        self.dcount[q] += 1
        sname = f"dma_{q}{idx}"
        self.count[sname] += 16
        val = self.count[sname]
        sem = self.sem[sname]
        self.ops[ename].append(
            lambda api, o=out, i=in_, s=sem, k=dict(kw):
            api.dma_start(out=o, in_=i, **k).then_inc(s, 16))
        self._update(reads, writes, (sname, val))

    def barrier(self):
        toks = [(n, self.count[n]) for n in self.ENGN]
        for q, n_ in self.NSEM.items():
            toks += [(f"dma_{q}{i}", self.count[f"dma_{q}{i}"])
                     for i in range(n_)]
        for n in self.ENGN:
            self._wait(n, toks)

    def finalize(self, block):
        assert all(not p for p in self._pending.values()), "pending un-inc'd ops"
        api_of = {"pe": block.tensor, "act": block.scalar, "dve": block.vector,
                  "pool": block.gpsimd, "sp": block.sync}
        for name in self.ENGN:
            ops = self.ops[name]
            if not ops:
                continue
            def body(api, ops=ops):
                for f in ops:
                    f(api)
            api_of[name](body)


class Core:
    pass


def emit_ln_tile(P, B, src_key, src_ap, dst_ap, dst_key, affine=None, sq=0):
    st = B.stats
    c = 8 * sq
    k = [f"st{sq}"]
    P.op("dve", lambda v: v.scalar_tensor_tensor(
        dst_ap, src_ap, 1.0, src_ap, ALU.mult, ALU.mult,
        accum_out=st[:, c + 0:c + 1]), reads=[src_key], writes=k + [dst_key])
    P.op("dve", lambda v: v.reduce_sum(st[:, c + 1:c + 2], src_ap, axis=AX.X),
         reads=[src_key], writes=k)
    P.op("dve", lambda v: v.tensor_scalar(st[:, c + 2:c + 3], st[:, c + 1:c + 2],
                                          1.0 / D, None, ALU.mult),
         reads=k, writes=k)
    P.op("dve", lambda v: v.scalar_tensor_tensor(
        st[:, c + 3:c + 4], st[:, c + 2:c + 3], -1.0, st[:, c + 2:c + 3],
        ALU.mult, ALU.mult), reads=k, writes=k)
    P.op("dve", lambda v: v.scalar_tensor_tensor(
        st[:, c + 3:c + 4], st[:, c + 0:c + 1], 1.0 / D, st[:, c + 3:c + 4],
        ALU.mult, ALU.add), reads=k, writes=k)
    P.op("dve", lambda v: v.tensor_scalar(st[:, c + 3:c + 4], st[:, c + 3:c + 4],
                                          LN_EPS, None, ALU.add),
         reads=k, writes=k)
    P.op("act", lambda s: s.activation(st[:, c + 4:c + 5], st[:, c + 3:c + 4],
                                       AF.Ln, bias=0.0, scale=1.0),
         reads=k, writes=k)
    P.op("act", lambda s: s.activation(st[:, c + 4:c + 5], st[:, c + 4:c + 5],
                                       AF.Exp, bias=0.0, scale=-0.5),
         reads=k, writes=k)
    P.op("dve", lambda v: v.scalar_tensor_tensor(
        st[:, c + 5:c + 6], st[:, c + 2:c + 3], -1.0, st[:, c + 4:c + 5],
        ALU.mult, ALU.mult), reads=k, writes=k)
    P.op("act", lambda s: s.activation(dst_ap, src_ap, AF.Identity,
                                       bias=st[:, c + 5:c + 6],
                                       scale=st[:, c + 4:c + 5]),
         reads=[src_key] + k, writes=[dst_key])
    if affine is not None:
        g_ap, g_key, b_ap, b_key = affine
        P.op("dve", lambda v: v.tensor_tensor(dst_ap, dst_ap, g_ap, ALU.mult),
             reads=[dst_key, g_key], writes=[dst_key])
        P.op("dve", lambda v: v.tensor_tensor(dst_ap, dst_ap, b_ap, ALU.add),
             reads=[dst_key, b_key], writes=[dst_key])


def emit_transpose8(P, B, src_ap_fn, src_key, dst_ap_fn, dst_key, bank=6):
    ps = B.ps[bank]
    bk = f"ps{bank}"
    for g in range(2):
        for j in range(4):
            c = g * 4 + j
            P.op("pe", lambda t, c=c, j=j: t.transpose(
                ps[:, j * 128:(j + 1) * 128], src_ap_fn(c), B.identf[:]),
                reads=[src_key, "identf"], writes=[bk], inc=(j == 3))
        for j in range(4):
            c = g * 4 + j
            P.op("act", lambda s, c=c, j=j: s.copy(
                dst_ap_fn(c), ps[:, j * 128:(j + 1) * 128]),
                reads=[bk], writes=[dst_key])


def emit_load_x(P, B, seq):
    src = B.xin_ap[seq * T:(seq + 1) * T, :].rearrange("(t p) d -> p t d", p=128)
    dst = B.xsb[:].rearrange("p (t d) -> p t d", d=D)
    P.dma("s", dst, src, reads=[], writes=["xsb"])


def emit_store_x(P, B, seq):
    dst = B.xout_ap[seq * T:(seq + 1) * T, :].rearrange("(t p) d -> p t d", p=128)
    src = B.xsb[:].rearrange("p (t d) -> p t d", d=D)
    P.dma("s", dst, src, reads=["xsb"], writes=["xout"])


def emit_ln2_residual(P, B, tile):
    emit_ln_tile(P, B, "ytile", B.ytile[:], B.aux[:], "aux", sq=tile % 4)
    P.op("dve", lambda v, tile=tile: v.tensor_tensor(
        B.xsb[:, tile * D:(tile + 1) * D], B.xsb[:, tile * D:(tile + 1) * D],
        B.aux[:], ALU.add), reads=["aux", "xsb"], writes=["xsb"])


def wkeys(lo, hi):
    return [f"wslab{q}" for q in range(4) if lo < (q + 1) * 1024 and hi > q * 1024]


def skeys(lo, hi):
    return [f"ssb{q}" for q in range(4) if lo < (q + 1) * 512 and hi > q * 512]


def stkeys(lo, hi):
    return [f"st{q}" for q in range(4) if lo < (q + 1) * 8 and hi > q * 8]


def emit_xnw_window(P, B, w):
    for tt in range(4):
        tile = w * 4 + tt
        src = B.xsb[:, tile * D:(tile + 1) * D]
        dstb, dstk = (B.ytile, "ytile") if tt % 2 == 0 else (B.aux, "aux")
        emit_ln_tile(P, B, "xsb", src, dstb[:], dstk, sq=tt)
        emit_transpose8(
            P, B, lambda c, dstb=dstb: dstb[:, c * 128:(c + 1) * 128], dstk,
            lambda c, tt=tt: B.xnw[:, c * 512 + tt * 128:c * 512 + (tt + 1) * 128],
            "xnw", bank=6 + tt % 2)


def emit_attn(P, B, layer):
    emit_attn_qkv(P, B, layer)
    emit_attn_scores(P, B, layer)
    emit_attn_oproj(P, B, layer)


def emit_attn_qkv(P, B, layer, tensors=(0, 1, 2)):
    a = f"a{layer}"
    for ti, wn in enumerate(("wq", "wk", "wv")):
        if ti not in tensors:
            continue
        wt_ap = B.aps[f"{a}_{wn}"]
        dst, dst_key = [(B.qa, "qa"), (B.kv, "kv"), (B.vv, "vv")][ti]
        if ti == 2 and layer == 1:
            P.dma("s", B.bcast[:], B.aps["a1_bvb"], reads=[], writes=["bcast"])
        for w in range(4):
            emit_xnw_window(P, B, w)
            fm_out = (ti < 2 and layer == 1)
            if ti < 2 and layer == 0:
                for sub in range(2):
                    for c in range(NC):
                        q = c % 2
                        P.dma("g", B.wslab[:, q * 1024:(q + 1) * 1024],
                              wt_ap[c * 128:(c + 1) * 128, :],
                              reads=[], writes=[f"wslab{q}"])
                        for k in range(4):
                            tt, hf = k // 2, k % 2
                            tl = sub * 2 + tt
                            P.op("pe", lambda t, c=c, tl=tl, hf=hf, pi=k, q=q:
                                 t.matmul(
                                     B.ps[pi][:],
                                     B.xnw[:, c * 512 + tl * 128:
                                           c * 512 + (tl + 1) * 128],
                                     B.wslab[:, q * 1024 + hf * 512:
                                             q * 1024 + (hf + 1) * 512],
                                     start=(c == 0), stop=(c == NC - 1)),
                                 reads=[f"wslab{q}", "xnw"], writes=[f"ps{k}"],
                                 inc=(k == 3))
                    for tt in range(2):
                        tile = w * 4 + sub * 2 + tt
                        for hf in range(2):
                            P.op("act", lambda s, tt=tt, hf=hf: s.copy(
                                B.ytile[:, hf * 512:(hf + 1) * 512],
                                B.ps[tt * 2 + hf][:]),
                                reads=[f"ps{tt * 2 + hf}"], writes=["ytile"])
                        P.dma("s", B.bcast[:],
                              B.aps["rope0_c"][tile * 128:(tile + 1) * 128, :],
                              reads=[], writes=["bcast"])
                        P.dma("s", B.ropeb[:],
                              B.aps["rope0_s"][tile * 128:(tile + 1) * 128, :],
                              reads=[], writes=["ropeb", "ropebB"])
                        P.op("dve", lambda v: v.tensor_tensor(
                            B.ssb[:, 0:1024], B.ytile[:], B.bcast[:], ALU.mult),
                            reads=["ytile", "bcast"], writes=["ssb0", "ssb1"])
                        yv = B.ytile[:].rearrange("p (d two) -> p d two", two=2)
                        sv = B.ropeb[:].rearrange("p (d two) -> p d two", two=2)
                        t2 = B.ssb[:, 1024:2048].rearrange("p (d two) -> p d two",
                                                           two=2)
                        P.op("dve", lambda v, yv=yv, sv=sv, t2=t2: v.tensor_tensor(
                            t2[:, :, 0], yv[:, :, 1], sv[:, :, 0], ALU.mult),
                            reads=["ytile", "ropeb", "ropebB"],
                            writes=["ssb2", "ssb3"])
                        P.op("dve", lambda v, yv=yv, sv=sv, t2=t2: v.tensor_tensor(
                            t2[:, :, 1], yv[:, :, 0], sv[:, :, 1], ALU.mult),
                            reads=["ytile", "ropeb", "ropebB"],
                            writes=["ssb2", "ssb3"])
                        P.op("dve", lambda v: v.tensor_tensor(
                            B.aux[:], B.ssb[:, 0:1024], B.ssb[:, 1024:2048],
                            ALU.add), reads=["ssb0", "ssb1", "ssb2", "ssb3"],
                            writes=["aux"])
                        emit_transpose8(
                            P, B, lambda c: B.aux[:, c * 128:(c + 1) * 128], "aux",
                            lambda c, tile=tile, dst=dst: dst[:, c * T + tile * 128:
                                                              c * T + (tile + 1) * 128],
                            dst_key)
                continue
            for c in range(NC):
                q = c % 2
                P.dma("g", B.wslab[:, q * 1024:(q + 1) * 1024],
                      wt_ap[c * 128:(c + 1) * 128, :],
                      reads=[], writes=[f"wslab{q}"])
                for k in range(8):
                    if fm_out:
                        P.op("pe", lambda t, c=c, m=k, q=q: t.matmul(
                            B.ps[m][:],
                            B.wslab[:, q * 1024 + m * 128:q * 1024 + (m + 1) * 128],
                            B.xnw[:, c * 512:(c + 1) * 512],
                            start=(c == 0), stop=(c == NC - 1)),
                            reads=[f"wslab{q}", "xnw"], writes=[f"ps{k}"],
                            inc=(k == 7))
                    else:
                        tt, hf = k // 2, k % 2
                        P.op("pe", lambda t, c=c, tt=tt, hf=hf, pi=k, q=q: t.matmul(
                            B.ps[pi][:],
                            B.xnw[:, c * 512 + tt * 128:c * 512 + (tt + 1) * 128],
                            B.wslab[:, q * 1024 + hf * 512:q * 1024 + (hf + 1) * 512],
                            start=(c == 0), stop=(c == NC - 1)),
                            reads=[f"wslab{q}", "xnw"], writes=[f"ps{k}"],
                            inc=(k == 7))
            if ti < 2 and layer == 1:
                for m in range(8):
                    P.dma("s", B.ropeb[:, 0:512],
                          B.aps["rope1_c"][m * 128:(m + 1) * 128,
                                           w * 512:(w + 1) * 512],
                          reads=[], writes=["ropeb"])
                    P.dma("s", B.ropeb[:, 512:1024],
                          B.aps["rope1_s"][m * 128:(m + 1) * 128,
                                           w * 512:(w + 1) * 512],
                          reads=[], writes=["ropebB"])
                    P.op("dve", lambda v, m=m, ti=ti: v.scalar_tensor_tensor(
                        B.ssb[:, 0:512], B.ps[m][:],
                        B.bcol[:, ti * 8 + m:ti * 8 + m + 1],
                        B.ropeb[:, 0:512], ALU.add, ALU.mult),
                        reads=[f"ps{m}", "ropeb", "bcol"], writes=["ssb0"])
                    P.op("dve", lambda v, m=m, ti=ti: v.scalar_tensor_tensor(
                        B.ssb[:, 512:1024], B.ps[m ^ 4][:],
                        B.bcol[:, ti * 8 + (m ^ 4):ti * 8 + (m ^ 4) + 1],
                        B.ropeb[:, 512:1024], ALU.add, ALU.mult),
                        reads=[f"ps{m ^ 4}", "ropebB", "bcol"], writes=["ssb1"])
                    P.op("dve", lambda v, m=m, w=w, dst=dst: v.tensor_tensor(
                        dst[:, m * T + w * 512:m * T + (w + 1) * 512],
                        B.ssb[:, 0:512], B.ssb[:, 512:1024], ALU.add),
                        reads=["ssb0", "ssb1"], writes=[dst_key])
            elif ti == 2:
                for tt in range(4):
                    tile = w * 4 + tt
                    for hf in range(2):
                        pi = tt * 2 + hf
                        vdst = B.vv[:, tile * D + hf * 512:tile * D + (hf + 1) * 512]
                        if layer == 1:
                            P.op("dve", lambda v, pi=pi, vdst=vdst, hf=hf:
                                 v.tensor_tensor(vdst, B.ps[pi][:],
                                                 B.bcast[:, hf * 512:(hf + 1) * 512],
                                                 ALU.add),
                                 reads=[f"ps{pi}", "bcast"], writes=["vv"])
                        else:
                            P.op("act", lambda s, pi=pi, vdst=vdst: s.copy(
                                vdst, B.ps[pi][:]),
                                reads=[f"ps{pi}"], writes=["vv"])


def emit_attn_scores(P, B, layer):
    ne = _HOST.get("ne")
    st = B.stats
    scale = 1.0 if layer == 1 else 0.125
    pwbufs = [(B.pw[:, 0:512], "pw"), (B.ropeb[:, 0:512], "ropeb"),
              (B.ropeb[:, 512:1024], "ropebB"), (B.bcast[:, 0:512], "bcast")]

    def win_plan(qt, h):
        wins = []
        for j in range((qt + 1 + 3) // 4):
            wid = min(512, (qt + 1) * 128 - j * 512)
            tiles = list(range(wid // 128))
            if layer == 1:
                tiles = [i for i in tiles if ne[h, qt, j * 4 + i]]
                if not tiles:
                    continue
            wins.append((j, wid, tiles))
        return wins

    def cnt_slot(h):
        if h % 2 == 0:
            return (B.wslab[:, 0:2048], ["wslab0", "wslab1"],
                    B.wslab[:, 2048:4096], ["wslab2", "wslab3"])
        return (B.xnw[:, 0:2048], ["xnw"], B.xnw[:, 2048:4096], ["xnw"])

    def stage_a0(qt, h, wins):
        c = h // 2
        base = 64 * (h % 2)
        cntb, cntk, biasb, biask = cnt_slot(h)
        if layer == 1:
            for (j, wid, tiles) in wins:
                P.dma("s", cntb[:, j * 512:j * 512 + wid],
                      B.aps["cnt"][h * T + qt * 128:h * T + (qt + 1) * 128,
                                   j * 512:j * 512 + wid],
                      reads=[], writes=cntk)
                P.dma("s", biasb[:, j * 512:j * 512 + wid],
                      B.aps["cbias"][h * T + qt * 128:h * T + (qt + 1) * 128,
                                     j * 512:j * 512 + wid],
                      reads=[], writes=biask)
        for (j, wid, tiles) in wins:
            P.op("pe", lambda t, j=j, wid=wid, c=c, base=base, qt=qt: t.matmul(
                B.ps[j][:, 0:wid],
                B.qa[base:base + 64, c * T + qt * 128:c * T + (qt + 1) * 128],
                B.kv[base:base + 64, c * T + j * 512:c * T + j * 512 + wid],
                start=True, stop=True),
                reads=["qa", "kv"], writes=[f"ps{j}"])

    def stage_a1(qt, h, wins, sbase):
        kext = (qt + 1) * 128
        slot = h % 2
        stb = 16 * slot
        stq = ["st0", "st1"] if slot == 0 else ["st2", "st3"]
        cntb, cntk, biasb, biask = cnt_slot(h)
        mc = 0
        for wi, (j, wid, tiles) in enumerate(wins):
            sb = sbase + j * 512
            sk = skeys(sb, sb + wid)
            if layer == 1:
                P.op("dve", lambda v, j=j, wid=wid, sb=sb, cntb=cntb:
                     v.tensor_tensor(
                         B.ssb[:, sb:sb + wid], B.ps[j][:, 0:wid],
                         cntb[:, j * 512:j * 512 + wid], ALU.mult),
                     reads=[f"ps{j}"] + cntk, writes=sk)
                P.op("dve", lambda v, j=j, wid=wid, sb=sb, biasb=biasb:
                     v.tensor_tensor(
                         B.ssb[:, sb:sb + wid], B.ssb[:, sb:sb + wid],
                         biasb[:, j * 512:j * 512 + wid], ALU.add),
                     reads=sk + biask, writes=sk)
            else:
                P.op("act", lambda s, j=j, wid=wid, sb=sb: s.copy(
                    B.ssb[:, sb:sb + wid], B.ps[j][:, 0:wid]),
                    reads=[f"ps{j}"], writes=sk)
                if j == wins[-1][0]:
                    off = sbase + kext - 128
                    P.op("dve", lambda v, off=off: v.tensor_tensor(
                        B.ssb[:, off:off + 128], B.ssb[:, off:off + 128],
                        B.trib[:], ALU.add),
                        reads=sk + ["trib"], writes=sk)
            P.op("dve", lambda v, wid=wid, sb=sb, mc=mc, stb=stb: v.reduce_max(
                st[:, stb + mc:stb + mc + 1], B.ssb[:, sb:sb + wid],
                axis=AX.X), reads=sk, writes=stq)
            mc += 1
        P.op("dve", lambda v, mc=mc, stb=stb: v.reduce_max(
            st[:, stb + 8:stb + 9], st[:, stb:stb + mc], axis=AX.X),
            reads=stq, writes=stq)
        P.op("dve", lambda v, stb=stb: v.tensor_scalar(
            st[:, stb + 9:stb + 10], st[:, stb + 8:stb + 9], -scale, None,
            ALU.mult), reads=stq, writes=stq)

    def stage_bx(qt, h, wins, sbase):
        slot = h % 2
        stb = 16 * slot
        stq = ["st0", "st1"] if slot == 0 else ["st2", "st3"]
        pvb = 4 + slot
        ptp = 6 + slot
        ptbb = B.ptb[:, slot * 512:slot * 512 + 512]
        ptk = f"ptb{slot}"
        first = True
        last = (wins[-1][0], wins[-1][2][-1])
        for wi, (j, wid, tiles) in enumerate(wins):
            sb = sbase + j * 512
            sk = skeys(sb, sb + wid)
            pwb, pwk = pwbufs[slot * 2 + wi % 2]
            P.op("act", lambda s, wid=wid, sb=sb, wi=wi, stb=stb, pwb=pwb,
                 sk=sk: s.activation(
                     pwb[:, 0:wid], B.ssb[:, sb:sb + wid], AF.Exp,
                     bias=st[:, stb + 9:stb + 10], scale=scale),
                 reads=sk + stq, writes=[pwk])
            P.op("dve", lambda v, wid=wid, wi=wi, stb=stb, pwb=pwb:
                 v.reduce_sum(st[:, stb + 10 + wi:stb + 11 + wi],
                              pwb[:, 0:wid], axis=AX.X),
                 reads=[pwk], writes=stq)
            for ii, i in enumerate(tiles):
                P.op("pe", lambda t, i=i, ii=ii, pwb=pwb, pwk=pwk, ptp=ptp:
                     t.transpose(
                         B.ps[ptp][:, ii * 128:(ii + 1) * 128],
                         pwb[:, i * 128:(i + 1) * 128], B.identf[:]),
                     reads=[pwk, "identf"], writes=[f"ps{ptp}"],
                     inc=(ii == len(tiles) - 1))
            P.op("dve", lambda v, n=len(tiles), ptbb=ptbb, ptp=ptp:
                 v.tensor_copy(ptbb[:, 0:n * 128], B.ps[ptp][:, 0:n * 128]),
                 reads=[f"ps{ptp}"], writes=[ptk])
            for ii, i in enumerate(tiles):
                kt = j * 4 + i
                isl = (j, i) == last
                P.op("pe", lambda t, ii=ii, kt=kt, h=h, ptbb=ptbb, pvb=pvb,
                     first=first, isl=isl: t.matmul(
                         B.ps[pvb][:, 0:64],
                         ptbb[:, ii * 128:(ii + 1) * 128],
                         B.vv[:, kt * D + h * 64:kt * D + (h + 1) * 64],
                         start=first, stop=isl),
                     reads=[ptk, "vv"], writes=[f"ps{pvb}"], inc=isl)
                first = False
        nw = len(wins)
        P.op("dve", lambda v, nw=nw, stb=stb: v.reduce_sum(
            st[:, stb + 8:stb + 9], st[:, stb + 10:stb + 10 + nw], axis=AX.X),
            reads=stq, writes=stq)
        P.op("dve", lambda v, stb=stb: v.reciprocal(
            st[:, stb + 9:stb + 10], st[:, stb + 8:stb + 9]),
            reads=stq, writes=stq)
        P.op("act", lambda s, h=h, stb=stb, pvb=pvb: s.activation(
            B.aux[:, h * 64:(h + 1) * 64], B.ps[pvb][:, 0:64], AF.Identity,
            bias=0.0, scale=st[:, stb + 9:stb + 10]),
            reads=[f"ps{pvb}"] + stq, writes=["aux"])

    depth1 = False
    for qt in range(NT):
        kext = (qt + 1) * 128
        small = kext <= 1024
        prev = None
        for h in range(H):
            wins = win_plan(qt, h)
            sbase = (h % 2) * 1024 if small else 0
            stage_a0(qt, h, wins)
            if depth1:
                stage_a1(qt, h, wins, sbase)
                stage_bx(qt, h, wins, sbase)
                continue
            if small:
                stage_a1(qt, h, wins, sbase)
                if prev is not None:
                    stage_bx(*prev)
            else:
                if prev is not None:
                    stage_bx(*prev)
                stage_a1(qt, h, wins, sbase)
            prev = (qt, h, wins, sbase)
        if not depth1:
            stage_bx(*prev)
        emit_transpose8(
            P, B, lambda c: B.aux[:, c * 128:(c + 1) * 128], "aux",
            lambda c, qt=qt: B.qa[:, c * T + qt * 128:c * T + (qt + 1) * 128],
            "qa")


def emit_attn_oproj(P, B, layer):
    a = f"a{layer}"
    wo_ap = B.aps[f"{a}_wo"]
    if layer == 1:
        P.dma("s", B.bcast[:], B.aps["a1_bob"], reads=[], writes=["bcast"])
    for tb in range(4):
        for c in range(NC):
            q = c % 2
            P.dma("g", B.wslab[:, q * 1024:(q + 1) * 1024],
                  wo_ap[c * 128:(c + 1) * 128, :], reads=[], writes=[f"wslab{q}"])
            for k in range(8):
                tt, hf = k // 2, k % 2
                tile = tb * 4 + tt
                P.op("pe", lambda t, c=c, hf=hf, pi=k, tile=tile, q=q: t.matmul(
                    B.ps[pi][:],
                    B.qa[:, c * T + tile * 128:c * T + (tile + 1) * 128],
                    B.wslab[:, q * 1024 + hf * 512:q * 1024 + (hf + 1) * 512],
                    start=(c == 0), stop=(c == NC - 1)),
                    reads=[f"wslab{q}", "qa"], writes=[f"ps{k}"], inc=(k == 7))
        for tt in range(4):
            tile = tb * 4 + tt
            for hf in range(2):
                pi = tt * 2 + hf
                if layer == 1:
                    P.op("dve", lambda v, pi=pi, hf=hf: v.tensor_tensor(
                        B.ytile[:, hf * 512:(hf + 1) * 512], B.ps[pi][:],
                        B.bcast[:, hf * 512:(hf + 1) * 512], ALU.add),
                        reads=[f"ps{pi}", "bcast"], writes=["ytile"])
                else:
                    P.op("act", lambda s, pi=pi, hf=hf: s.copy(
                        B.ytile[:, hf * 512:(hf + 1) * 512], B.ps[pi][:]),
                        reads=[f"ps{pi}"], writes=["ytile"])
            emit_ln2_residual(P, B, tile)


def emit_f0(P, B):
    st = B.stats
    P.dma("s", B.ssb[:, 0:1024], B.aps["f0_gb"], reads=[], writes=["ssb0", "ssb1"])
    P.dma("s", B.ssb[:, 1024:2048], B.aps["f0_bb"], reads=[], writes=["ssb2", "ssb3"])
    P.dma("s", B.scol[:, 0:1], B.aps["f0_sc"], reads=[], writes=["scol"])
    for tile in range(NT):
        src = B.xsb[:, tile * D:(tile + 1) * D]
        emit_ln_tile(P, B, "xsb", src, B.ytile[:], "ytile")
        emit_ln_tile(P, B, "ytile", B.ytile[:], B.aux[:], "aux",
                     affine=(B.ssb[:, 0:1024], "ssb0", B.ssb[:, 1024:2048], "ssb2"))
        P.op("act", lambda s, tile=tile: s.activation(
            B.qa[:, tile * D:(tile + 1) * D], B.aux[:], AF.Identity, bias=0.0,
            scale=1.0), reads=["aux"], writes=["qa"])
        P.op("dve", lambda v: v.scalar_tensor_tensor(
            B.ytile[:], B.aux[:], 1.0, B.aux[:], ALU.mult, ALU.mult,
            accum_out=st[:, 0:1]), reads=["aux"], writes=["st0", "st1", "ytile"])
        P.op("dve", lambda v: v.tensor_scalar(st[:, 0:1], st[:, 0:1], 1e-30,
                                              None, ALU.add),
             reads=["st0", "st1"], writes=["st0", "st1"])
        P.op("act", lambda s: s.activation(st[:, 1:2], st[:, 0:1], AF.Ln,
                                           bias=0.0, scale=1.0),
             reads=["st0", "st1"], writes=["st0", "st1"])
        P.op("act", lambda s: s.activation(st[:, 1:2], st[:, 1:2], AF.Exp,
                                           bias=0.0, scale=0.5),
             reads=["st0", "st1"], writes=["st0", "st1"])
        P.op("dve", lambda v: v.tensor_scalar_max(st[:, 1:2], st[:, 1:2], 1e-12),
             reads=["st0", "st1"], writes=["st0", "st1"])
        P.op("dve", lambda v: v.reciprocal(st[:, 2:3], st[:, 1:2]),
             reads=["st0", "st1"], writes=["st0", "st1"])
        P.op("dve", lambda v: v.tensor_tensor(st[:, 3:4], st[:, 2:3],
                                              B.scol[:, 0:1], ALU.mult),
             reads=["st0", "st1", "scol"], writes=["st0", "st1"])
        P.op("act", lambda s: s.activation(B.ytile[:], B.aux[:], AF.Identity,
                                           bias=0.0, scale=st[:, 3:4]),
             reads=["aux", "st0", "st1"], writes=["ytile"])
        emit_transpose8(
            P, B, lambda c: B.ytile[:, c * 128:(c + 1) * 128], "ytile",
            lambda c, tile=tile: B.kv[:, c * T + tile * 128:c * T + (tile + 1) * 128],
            "kv")

    for c in range(NC):
        P.dma("s", B.aux[:, 0:256], B.aps["f0_mu"][c * 128:(c + 1) * 128, :],
              reads=["aux"], writes=["aux"])
        for half in range(2):
            P.op("pe", lambda t, half=half: t.transpose(
                B.ps[6][:, half * 128:(half + 1) * 128],
                B.aux[:, half * 128:(half + 1) * 128], B.identf[:]),
                reads=["aux", "identf"], writes=["ps6"], inc=(half == 1))
        for half in range(2):
            P.op("act", lambda s, c=c, half=half: s.copy(
                B.ssb[:, half * 1024 + c * 128:half * 1024 + (c + 1) * 128],
                B.ps[6][:, half * 128:(half + 1) * 128]),
                reads=["ps6"], writes=skeys(half * 1024, (half + 1) * 1024))
    for half in range(2):
        mt = B.ssb[:, half * 1024:(half + 1) * 1024]
        P.op("dve", lambda v, mt=mt: v.scalar_tensor_tensor(
            B.aux[:], mt, 1.0, mt, ALU.mult, ALU.mult, accum_out=st[:, 4:5]),
            reads=skeys(half * 1024, (half + 1) * 1024), writes=["st0", "st1", "aux"])
        P.op("dve", lambda v: v.tensor_scalar(st[:, 4:5], st[:, 4:5], 1e-30,
                                              None, ALU.add),
             reads=["st0", "st1"], writes=["st0", "st1"])
        P.op("act", lambda s: s.activation(st[:, 5:6], st[:, 4:5], AF.Ln,
                                           bias=0.0, scale=1.0),
             reads=["st0", "st1"], writes=["st0", "st1"])
        P.op("act", lambda s: s.activation(st[:, 5:6], st[:, 5:6], AF.Exp,
                                           bias=0.0, scale=0.5),
             reads=["st0", "st1"], writes=["st0", "st1"])
        P.op("dve", lambda v: v.tensor_scalar_max(st[:, 5:6], st[:, 5:6], 1e-12),
             reads=["st0", "st1"], writes=["st0", "st1"])
        P.op("dve", lambda v: v.reciprocal(st[:, 6:7], st[:, 5:6]),
             reads=["st0", "st1"], writes=["st0", "st1"])
        P.op("act", lambda s, mt=mt: s.activation(mt, mt, AF.Identity, bias=0.0,
                                                  scale=st[:, 6:7]),
             reads=skeys(half * 1024, (half + 1) * 1024) + ["st0", "st1"],
             writes=skeys(half * 1024, (half + 1) * 1024))
        for c in range(NC):
            P.op("pe", lambda t, half=half, c=c: t.transpose(
                B.ps[6][:, 0:128],
                B.ssb[:, half * 1024 + c * 128:half * 1024 + (c + 1) * 128],
                B.identf[:]),
                reads=skeys(half * 1024, (half + 1) * 1024) + ["identf"],
                writes=["ps6"])
            P.op("act", lambda s, half=half, c=c: s.copy(
                B.mub[:, c * 256 + half * 128:c * 256 + (half + 1) * 128],
                B.ps[6][:, 0:128]), reads=["ps6"], writes=["xnw"])

    for half in range(2):
        for w in range(4):
            for c in range(NC):
                P.op("pe", lambda t, half=half, w=w, c=c: t.matmul(
                    B.ps[w][:],
                    B.mub[:, c * 256 + half * 128:c * 256 + (half + 1) * 128],
                    B.kv[:, c * T + w * 512:c * T + (w + 1) * 512],
                    start=(c == 0), stop=(c == NC - 1)),
                    reads=["xnw", "kv"], writes=[f"ps{w}"], inc=(c == NC - 1))
        for w in range(4):
            P.op("dve", lambda v, w=w: v.reduce_max(
                st[:, 8 + w:9 + w], B.ps[w][:], axis=AX.X),
                reads=[f"ps{w}"], writes=["st1"])
        P.op("dve", lambda v: v.reduce_max(st[:, 12:13], st[:, 8:12], axis=AX.X),
             reads=["st0", "st1"], writes=["st0", "st1"])
        P.op("dve", lambda v: v.tensor_scalar(st[:, 13:14], st[:, 12:13], -1.0,
                                              None, ALU.mult),
             reads=["st0", "st1"], writes=["st0", "st1"])
        for w in range(4):
            P.op("act", lambda s, w=w: s.activation(
                B.ssb[:, w * 512:(w + 1) * 512], B.ps[w][:], AF.Exp,
                bias=st[:, 13:14], scale=1.0, accum_out=st[:, 8 + w:9 + w]),
                reads=[f"ps{w}", "st0", "st1"],
                writes=skeys(w * 512, (w + 1) * 512) + ["st0", "st1"])
        P.op("dve", lambda v: v.reduce_sum(st[:, 12:13], st[:, 8:12], axis=AX.X),
             reads=["st0", "st1"], writes=["st0", "st1"])
        P.op("dve", lambda v: v.reciprocal(st[:, 13:14], st[:, 12:13]),
             reads=["st0", "st1"], writes=["st0", "st1"])
        P.op("act", lambda s: s.activation(B.ssb[:], B.ssb[:], AF.Identity,
                                           bias=0.0, scale=st[:, 13:14]),
             reads=["ssb0", "ssb1", "ssb2", "ssb3", "st0", "st1"],
             writes=["ssb0", "ssb1", "ssb2", "ssb3"])
        for tile in range(NT):
            P.op("pe", lambda t, tile=tile: t.transpose(
                B.ps[6][:, 0:128], B.ssb[:, tile * 128:(tile + 1) * 128],
                B.identf[:]),
                reads=skeys(tile * 128, (tile + 1) * 128) + ["identf"],
                writes=["ps6"])
            P.op("act", lambda s, tile=tile, half=half: s.copy(
                B.vv[:, tile * 256 + half * 128:tile * 256 + (half + 1) * 128],
                B.ps[6][:, 0:128]), reads=["ps6"], writes=["vv"])

    for tile in range(NT):
        for c in range(NC):
            P.op("pe", lambda t, tile=tile, c=c: t.matmul(
                B.ps[0][:, 0:256],
                B.kv[:, c * T + tile * 128:c * T + (tile + 1) * 128],
                B.mub[:, c * 256:(c + 1) * 256],
                start=(c == 0), stop=(c == NC - 1)),
                reads=["kv", "xnw"], writes=["ps0"], inc=(c == NC - 1))
        P.op("dve", lambda v: v.reduce_max(st[:, 8:9], B.ps[0][:, 0:256],
                                           axis=AX.X),
             reads=["ps0"], writes=["st1"])
        P.op("dve", lambda v: v.tensor_scalar(st[:, 9:10], st[:, 8:9], -1.0,
                                              None, ALU.mult),
             reads=["st0", "st1"], writes=["st0", "st1"])
        P.op("act", lambda s: s.activation(B.aux[:, 0:256], B.ps[0][:, 0:256],
                                           AF.Exp, bias=st[:, 9:10], scale=1.0,
                                           accum_out=st[:, 8:9]),
             reads=["ps0", "st0", "st1"], writes=["aux", "st0", "st1"])
        P.op("dve", lambda v: v.reciprocal(st[:, 9:10], st[:, 8:9]),
             reads=["st0", "st1"], writes=["st0", "st1"])
        P.op("act", lambda s: s.activation(B.aux[:, 0:256], B.aux[:, 0:256],
                                           AF.Identity, bias=0.0, scale=st[:, 9:10]),
             reads=["aux", "st0", "st1"], writes=["aux"])
        for half in range(2):
            P.op("pe", lambda t, half=half: t.transpose(
                B.ps[6][:, 0:128], B.aux[:, half * 128:(half + 1) * 128],
                B.identf[:]), reads=["aux", "identf"], writes=["ps6"])
            P.op("act", lambda s, half=half, tile=tile: s.copy(
                B.vv[:, 4096 + half * 2048 + tile * 128:
                      4096 + half * 2048 + (tile + 1) * 128],
                B.ps[6][:, 0:128]), reads=["ps6"], writes=["vv"])

    for c in range(NC):
        pi = c % 2
        for tile in range(NT):
            P.op("pe", lambda t, c=c, tile=tile, pi=pi: t.matmul(
                B.ps[pi][:, 0:256],
                B.qa[:, tile * D + c * 128:tile * D + (c + 1) * 128],
                B.vv[:, tile * 256:(tile + 1) * 256],
                start=(tile == 0), stop=(tile == NT - 1)),
                reads=["qa", "vv"], writes=[f"ps{pi}"], inc=(tile == NT - 1))
        P.op("act", lambda s, c=c, pi=pi: s.copy(
            B.slotin[:, c * 256:(c + 1) * 256], B.ps[pi][:, 0:256]),
            reads=[f"ps{pi}"], writes=["xnw"])

    for e in range(16):
        for c in range(NC):
            lo = (c % 4) * 256 + (2048 if c >= 4 else 0)
            P.dma("g", B.wslab[:, lo:lo + 256],
                  B.aps["f0_w1"][e * D + c * 128:e * D + (c + 1) * 128, :],
                  reads=[], writes=wkeys(lo, lo + 256))
        for c in range(NC):
            lo = (c % 4) * 256 + (2048 if c >= 4 else 0)
            P.op("pe", lambda t, e=e, c=c, lo=lo: t.matmul(
                B.ps[2][0:16, 0:256],
                B.slotin[:, c * 256 + e * 16:c * 256 + (e + 1) * 16],
                B.wslab[:, lo:lo + 256],
                start=(c == 0), stop=(c == NC - 1)),
                reads=["xnw"] + wkeys(lo, lo + 256), writes=["ps2"],
                inc=(c == NC - 1))
        P.dma("s", B.hsb[0:16, 0:256], B.aps["f0_b1b"][e * 16:(e + 1) * 16, :],
              reads=["pw"], writes=["pw"])
        P.op("dve", lambda v: v.tensor_tensor(
            B.hsb[0:16, 0:256], B.ps[2][0:16, 0:256], B.hsb[0:16, 0:256],
            ALU.add), reads=["ps2", "pw"], writes=["pw"])
        P.op("act", lambda s: s.activation(B.hsb[0:16, 256:512],
                                           B.hsb[0:16, 0:256], AF.Gelu,
                                           bias=0.0, scale=1.0),
             reads=["pw"], writes=["pw"])
        for half in range(2):
            P.op("pe", lambda t, half=half: t.transpose(
                B.ps[3][:, 0:16],
                B.hsb[0:16, 256 + half * 128:256 + (half + 1) * 128],
                B.identf[0:16, 0:16]),
                reads=["pw", "identf"], writes=["ps3"])
            P.op("act", lambda s, half=half: s.copy(
                B.htb[:, half * 16:(half + 1) * 16], B.ps[3][:, 0:16]),
                reads=["ps3"], writes=["ptb0"])
        for half in range(2):
            lo = half * 1024
            P.dma("g", B.wslab[:, lo:lo + 1024],
                  B.aps["f0_w2"][e * 256 + half * 128:e * 256 + (half + 1) * 128, :],
                  reads=[], writes=wkeys(lo, lo + 1024))
        P.dma("s", B.aux[0:16, :], B.aps["f0_b2b"][e * 16:(e + 1) * 16, :],
              reads=["aux"], writes=["aux"])
        for dh in range(2):
            for half in range(2):
                lo = half * 1024
                P.op("pe", lambda t, dh=dh, half=half, lo=lo: t.matmul(
                    B.ps[2][0:16, 0:512],
                    B.htb[:, half * 16:(half + 1) * 16],
                    B.wslab[:, lo + dh * 512:lo + (dh + 1) * 512],
                    start=(half == 0), stop=(half == 1)),
                    reads=["ptb0"] + wkeys(lo, lo + 1024),
                    writes=["ps2"], inc=(half == 1))
            P.op("dve", lambda v, dh=dh: v.tensor_tensor(
                B.aux[0:16, dh * 512:(dh + 1) * 512], B.ps[2][0:16, 0:512],
                B.aux[0:16, dh * 512:(dh + 1) * 512], ALU.add),
                reads=["ps2", "aux"], writes=["aux"])
        P.dma("g", B.vv[16 * (e % 8):16 * (e % 8) + 16,
                        8192 + (e // 8) * 1024:8192 + (e // 8 + 1) * 1024],
              B.aux[0:16, 0:1024], reads=["aux"], writes=["vv"])

    for tile in range(NT):
        for dh in range(2):
            pi = dh
            for ch in range(2):
                P.op("pe", lambda t, tile=tile, dh=dh, ch=ch, pi=pi: t.matmul(
                    B.ps[pi][:],
                    B.vv[:, 4096 + ch * 2048 + tile * 128:
                          4096 + ch * 2048 + (tile + 1) * 128],
                    B.vv[:, 8192 + ch * 1024 + dh * 512:
                          8192 + ch * 1024 + (dh + 1) * 512],
                    start=(ch == 0), stop=(ch == 1)),
                    reads=["vv"], writes=[f"ps{pi}"], inc=(ch == 1))
            P.op("act", lambda s, dh=dh, pi=pi: s.copy(
                B.ytile[:, dh * 512:(dh + 1) * 512], B.ps[pi][:]),
                reads=[f"ps{pi}"], writes=["ytile"])
        emit_ln2_residual(P, B, tile)


def emit_f1_cast(P, B):
    for nm in ("w1", "w2", "h1", "h2", "w3"):
        src = B.aps[f"f1_{nm}"]
        dst = B.scr[nm]
        rows = src.shape[0]
        step = rows // 8
        for i in range(8):
            P.dma("g", dst[i * step:(i + 1) * step, :],
                  src[i * step:(i + 1) * step, :],
                  reads=[], writes=[f"scr_{nm}"])


def emit_f1(P, B):
    st = B.stats
    P.dma("s", B.scol[:, 0:32], B.aps["f1_w1_s"], reads=["scol"], writes=["scol"])
    P.dma("s", B.scol[:, 32:64], B.aps["f1_w2_s"], reads=["scol"], writes=["scol"])
    P.dma("s", B.scol2[:, 0:32], B.aps["f1_h1_s"], reads=["scol2"], writes=["scol2"])
    P.dma("s", B.scol2[:, 32:64], B.aps["f1_h2_s"], reads=["scol2"], writes=["scol2"])
    P.dma("s", B.bcast[:], B.aps["f1_w3_sb"], reads=["bcast"], writes=["bcast"])
    for w in range(4):
        emit_xnw_window(P, B, w)
        for m in range(32):
            par = m % 2
            off = 2048 * par
            k1, k2 = f"wslab{2 * par}", f"wslab{2 * par + 1}"
            sso = 1024 * par
            psa, psb = 2 * par, 2 * par + 1
            P.dma("g" if par == 0 else "s", B.wslab[:, off:off + 1024],
                  B.scr["w1"][m * 128:(m + 1) * 128, :],
                  reads=["scr_w1"], writes=[k1])
            P.dma("g" if par == 0 else "s", B.wslab[:, off + 1024:off + 2048],
                  B.scr["w2"][m * 128:(m + 1) * 128, :],
                  reads=["scr_w2"], writes=[k2])
            for c in range(NC):
                P.op("pe", lambda t, c=c, off=off, psa=psa, k1=k1: t.matmul(
                    B.ps[psa][:], B.wslab[:, off + c * 128:off + (c + 1) * 128],
                    B.xnw[:, c * 512:(c + 1) * 512],
                    start=(c == 0), stop=(c == NC - 1)),
                    reads=[k1, "xnw"], writes=[f"ps{psa}"], inc=(c == NC - 1))
                P.op("pe", lambda t, c=c, off=off, psb=psb, k2=k2: t.matmul(
                    B.ps[psb][:],
                    B.wslab[:, off + 1024 + c * 128:off + 1024 + (c + 1) * 128],
                    B.xnw[:, c * 512:(c + 1) * 512],
                    start=(c == 0), stop=(c == NC - 1)),
                    reads=[k2, "xnw"], writes=[f"ps{psb}"], inc=(c == NC - 1))
            P.op("act", lambda s, m=m, psa=psa, sso=sso: s.activation(
                B.ssb[:, sso:sso + 512], B.ps[psa][:], AF.Silu, bias=0.0,
                scale=B.scol[:, m:m + 1]),
                reads=[f"ps{psa}", "scol"], writes=skeys(sso, sso + 512))
            P.op("act", lambda s, m=m, psb=psb, sso=sso: s.activation(
                B.ssb[:, sso + 512:sso + 1024], B.ps[psb][:], AF.Identity,
                bias=0.0, scale=B.scol[:, 32 + m:33 + m]),
                reads=[f"ps{psb}", "scol"], writes=skeys(sso + 512, sso + 1024))
            P.op("dve", lambda v, m=m, sso=sso: v.tensor_tensor(
                B.qa[:, m * 512:(m + 1) * 512], B.ssb[:, sso:sso + 512],
                B.ssb[:, sso + 512:sso + 1024], ALU.mult),
                reads=skeys(sso, sso + 1024), writes=["qa"])
        for (wname, src_key, srcbuf, dst_key, dstbuf, scidx) in (
                ("h1", "qa", B.qa, "kv", B.kv, 0),
                ("h2", "kv", B.kv, "qa", B.qa, 32)):
            for m in range(32):
                par = m % 2
                slab = B.wslab if par == 0 else B.xnw
                slabk = (["wslab0", "wslab1", "wslab2", "wslab3"]
                         if par == 0 else ["xnw"])
                P.dma("g" if par == 0 else "s", slab[:, 0:4096],
                      B.scr[wname][m * 128:(m + 1) * 128, :],
                      reads=[f"scr_{wname}"], writes=slabk)
                pi = m % 2
                for i in range(32):
                    P.op("pe", lambda t, i=i, pi=pi, srcbuf=srcbuf, slab=slab,
                         slabk=slabk: t.matmul(
                             B.ps[pi][:], slab[:, i * 128:(i + 1) * 128],
                             srcbuf[:, i * 512:(i + 1) * 512],
                             start=(i == 0), stop=(i == 31)),
                        reads=slabk + [src_key],
                        writes=[f"ps{pi}"], inc=(i == 31))
                P.op("act", lambda s, m=m, pi=pi, dstbuf=dstbuf, scidx=scidx:
                     s.activation(dstbuf[:, m * 512:(m + 1) * 512], B.ps[pi][:],
                                  AF.Gelu, bias=0.0,
                                  scale=B.scol2[:, scidx + m:scidx + m + 1]),
                     reads=[f"ps{pi}", "scol2"], writes=[dst_key])
        for i in range(32):
            off = 1024 * (i % 2)
            wk = f"wslab{i % 2}"
            P.dma("g" if i % 2 == 0 else "s", B.wslab[:, off:off + 1024],
                  B.scr["w3"][i * 128:(i + 1) * 128, :],
                  reads=["scr_w3"], writes=[wk])
            for k in range(8):
                tt, hf = k // 2, k % 2
                P.op("pe", lambda t, i=i, tt=tt, hf=hf, pi=k, off=off, wk=wk:
                     t.matmul(
                         B.ps[pi][:],
                         B.qa[:, i * 512 + tt * 128:i * 512 + (tt + 1) * 128],
                         B.wslab[:, off + hf * 512:off + (hf + 1) * 512],
                         start=(i == 0), stop=(i == 31)),
                    reads=[wk, "qa"], writes=[f"ps{k}"], inc=(k == 7))
        for tt in range(4):
            tile = w * 4 + tt
            for hf in range(2):
                pi = tt * 2 + hf
                P.op("dve", lambda v, pi=pi, hf=hf: v.tensor_tensor(
                    B.ytile[:, hf * 512:(hf + 1) * 512], B.ps[pi][:],
                    B.bcast[:, hf * 512:(hf + 1) * 512], ALU.mult),
                    reads=[f"ps{pi}", "bcast"], writes=["ytile"])
            emit_ln2_residual(P, B, tile)


def build_nc(parts):
    import contextlib
    nc = bass.Bass("TRN2", target_bir_lowering=False)
    ctx = contextlib.ExitStack()
    P = Prog(nc, ctx)
    B = Core()
    B.aps = {}
    B.scr = {}

    def din(name, shape, dtype=F32):
        B.aps[name] = nc.dram_tensor(name, list(shape), dtype,
                                     kind="ExternalInput").ap()
        return B.aps[name]

    din("xin", (SEQS * T, D))
    B.xin_ap = B.aps["xin"]
    B.xout_ap = nc.dram_tensor("xout", [SEQS * T, D], F32,
                               kind="ExternalOutput").ap()
    din("identf", (128, 128))
    din("trib", (128, 128), BF16)
    if "a0" in parts:
        for nm in ("wq", "wk", "wv", "wo"):
            din(f"a0_{nm}", (D, D))
        din("rope0_c", (T, D))
        din("rope0_s", (T, D))
    if "a1" in parts:
        for nm in ("wq", "wk", "wv", "wo"):
            din(f"a1_{nm}", (D, D))
        for nm in ("bq", "bk", "bv", "bo"):
            din(f"a1_{nm}c", (128, NC))
        din("a1_bvb", (128, D))
        din("a1_bob", (128, D))
        din("rope1_c", (D, T))
        din("rope1_s", (D, T))
        din("cnt", (H * T, T), BF16)
        din("cbias", (H * T, T), BF16)
    if "f0" in parts:
        din("f0_mu", (D, 256))
        din("f0_gb", (128, D))
        din("f0_bb", (128, D))
        din("f0_sc", (128, 1))
        din("f0_w1", (16 * D, 256))
        din("f0_w2", (16 * 256, D))
        din("f0_b1b", (256, 256))
        din("f0_b2b", (256, D))
    if "f1" in parts:
        din("f1_w1", (32 * 128, 8 * 128))
        din("f1_w2", (32 * 128, 8 * 128))
        din("f1_h1", (32 * 128, 32 * 128))
        din("f1_h2", (32 * 128, 32 * 128))
        din("f1_w3", (DH, D))
        for nm in ("w1_s", "w2_s", "h1_s", "h2_s"):
            din(f"f1_{nm}", (128, 32))
        din("f1_w3_sb", (128, D))
        for nm, shape in (("w1", (32 * 128, 8 * 128)), ("w2", (32 * 128, 8 * 128)),
                          ("h1", (32 * 128, 32 * 128)),
                          ("h2", (32 * 128, 32 * 128)), ("w3", (DH, D))):
            B.scr[nm] = nc.dram_tensor("scr_" + nm, list(shape), BF16,
                                       kind="Internal").ap()

    B.xsb = P.sbuf("xsb", [128, NT * D], F32)
    B.qa = P.sbuf("qa", [128, 8 * T], BF16)
    B.kv = P.sbuf("kv", [128, 8 * T], BF16)
    B.vv = P.sbuf("vv", [128, NT * D], BF16)
    B.wslab = P.sbuf("wslab", [128, 4096], BF16)
    B.xnw = P.sbuf("xnw", [128, 4096], BF16)
    B.ropeb = P.sbuf("ropeb", [128, 1024], F32)
    B.ssb = P.sbuf("ssb", [128, 2048], F32)
    B.pw = P.sbuf("pw", [128, 512], F32)
    B.ptb = P.sbuf("ptb0", [128, 1024], BF16)
    B.ytile = P.sbuf("ytile", [128, 1024], F32)
    B.aux = P.sbuf("aux", [128, 1024], F32)
    B.stats = P.sbuf("stats", [128, 32], F32)
    B.bcast = P.sbuf("bcast", [128, 1024], F32)
    B.scol = P.sbuf("scol", [128, 64], F32)
    B.scol2 = P.sbuf("scol2", [128, 64], F32)
    B.hsb = B.pw
    B.htb = B.ptb
    B.mub = B.xnw
    B.slotin = B.mub
    B.identf = P.sbuf("identf_sb", [128, 128], F32)
    B.trib = P.sbuf("trib_sb", [128, 128], BF16)
    B.bcol = P.sbuf("bcol", [128, 16], F32)
    B.ps = [nc.alloc_psum_tensor(f"ps{i}", [128, 512], F32) for i in range(8)]
    assert P.sbuf_bytes <= 211000, f"SBUF over budget: {P.sbuf_bytes}"

    with nc.Block() as block:
        P.dma("s", B.identf[:], B.aps["identf"], writes=["identf"])
        P.dma("s", B.trib[:], B.aps["trib"], writes=["trib"])
        if "f1" in parts:
            emit_f1_cast(P, B)
        for seq in range(SEQS):
            emit_load_x(P, B, seq)
            for part in parts:
                if part == "a0":
                    emit_attn(P, B, 0)
                elif part == "a1":
                    P.dma("s", B.bcol[:, 0:8], B.aps["a1_bqc"], writes=["bcol"])
                    P.dma("s", B.bcol[:, 8:16], B.aps["a1_bkc"], writes=["bcol"])
                    emit_attn(P, B, 1)
                elif part == "f0":
                    emit_f0(P, B)
                elif part == "f1":
                    emit_f1(P, B)
            emit_store_x(P, B, seq)
        P.barrier()
        P.finalize(block)
    return nc, ctx


def _in_map_for(parts, w, tabs, xin):
    m = {"xin": xin, "identf": tabs["identf"], "trib": tabs["trib"]}
    if "a0" in parts:
        for nm in ("wq", "wk", "wv", "wo"):
            m[f"a0_{nm}"] = w[f"a0_{nm}"]
        m["rope0_c"] = tabs["rope0_c"]
        m["rope0_s"] = tabs["rope0_s"]
    if "a1" in parts:
        for nm in ("wq", "wk", "wv", "wo"):
            m[f"a1_{nm}"] = w[f"a1_{nm}"]
        for nm in ("bq", "bk", "bv", "bo"):
            m[f"a1_{nm}c"] = w[f"a1_{nm}c"]
        m["a1_bvb"] = w["a1_bvb"]
        m["a1_bob"] = w["a1_bob"]
        m["rope1_c"] = tabs["rope1_c"]
        m["rope1_s"] = tabs["rope1_s"]
        m["cnt"] = tabs["cnt"]
        m["cbias"] = tabs["cbias"]
    if "f0" in parts:
        for nm in ("mu", "gb", "bb", "sc", "w1", "w2", "b1b", "b2b"):
            m[f"f0_{nm}"] = w[f"f0_{nm}"]
    if "f1" in parts:
        for nm in ("w1", "w2", "h1", "h2", "w3", "w1_s", "w2_s", "h1_s",
                   "h2_s", "w3_sb"):
            m[f"f1_{nm}"] = w[f"f1_{nm}"]
    return m


def _run_parts(parts_list, w, tabs, x):
    cur = np.ascontiguousarray(x.reshape(16 * T, D), dtype=np.float32)
    for parts in parts_list:
        nc, ctx = build_nc(parts)
        in_maps = []
        for c in range(NCORES):
            xin = np.ascontiguousarray(cur[c * SEQS * T:(c + 1) * SEQS * T, :])
            in_maps.append(_in_map_for(parts, w, tabs, xin))
        res = run_bass_kernel_spmd(nc, in_maps, core_ids=list(range(NCORES)))
        ctx.close()
        cur = np.concatenate([res.results[c]["xout"] for c in range(NCORES)],
                             axis=0)
    return cur.reshape(16, T, D)


def kernel(**inputs):
    tabs = _host_tables()
    w = _prep_weights(inputs)
    x = np.asarray(inputs["L0_attn_x"], dtype=np.float32)
    out = _run_parts([("a0", "f0", "a1", "f1")], w, tabs, x)
    return out.astype(np.float32)
